# version 14
# baseline (speedup 1.0000x reference)
import contextlib
import numpy as np
import concourse.bass as bass
import concourse.mybir as mybir

F32 = mybir.dt.float32
BF16 = mybir.dt.bfloat16
I32 = mybir.dt.int32
AF = mybir.ActivationFunctionType
ALU = mybir.AluOpType
AX = mybir.AxisListType

S, D, DI, H = 4096, 1024, 4096, 3072
NT = S // 128
NC = D // 128
NH = H // 128
EPS = 1e-5

ENGINES = ("sync", "gpsimd", "vector", "scalar", "tensor")


class Builder:
    def __init__(self, nc, stack):
        self.nc = nc
        self.prog = {e: [] for e in ENGINES}
        self.sems = {}
        self.cnt = {}
        self.last_wait = {}
        self.auto_chain = {"scalar": True, "vector": True, "gpsimd": True}
        self._stack = stack

    def sem(self, name):
        h = self._stack.enter_context(self.nc.semaphore(name))
        self.sems[name] = h
        self.cnt[name] = 0

    def emit(self, eng, fn):
        self.prog[eng].append(fn)

    def inc(self, name, k):
        self.cnt[name] += k
        return self.cnt[name]

    def cur(self, name):
        return self.cnt[name]

    def wait(self, eng, name, tok):
        if tok is None or tok <= 0:
            return
        key = (eng, name)
        if self.last_wait.get(key, 0) >= tok:
            return
        self.last_wait[key] = tok
        h = self.sems[name]
        self.emit(eng, lambda e, h=h, tok=tok: e.wait_ge(h, tok))

    def dma(self, eng, out_ap, in_ap, sem):
        tok = self.inc(sem, 16)
        h = self.sems[sem]
        self.emit(eng, lambda e, o=out_ap, i=in_ap, h=h: e.dma_start(out=o, in_=i).then_inc(h, 16))
        return tok

    ENG_SEM = {"scalar": "SE", "vector": "VE", "gpsimd": "GE", "tensor": "TE"}

    def op(self, eng, fn, sem=None):
        own = self.ENG_SEM.get(eng)
        if sem is not None and own is not None:
            assert sem == own, (eng, sem)
        if own is None:
            self.emit(eng, fn)
            return None
        if self.auto_chain.get(eng, False):
            prev = self.cnt[own]
            self.wait(eng, own, prev)
        tok = self.inc(own, 1)
        h = self.sems[own]
        self.emit(eng, lambda e, fn=fn, h=h: fn(e).then_inc(h, 1))
        return tok

    def drain(self, eng):
        self.emit(eng, lambda e: e.drain())


class Ring:

    def __init__(self, b, name, aps):
        self.b = b
        self.name = name
        self.aps = list(aps)
        self.n = len(self.aps)
        self.snames = []
        for i in range(self.n):
            sn = f"{name}{i}"
            b.sem(sn)
            self.snames.append(sn)
        self.free = [None] * self.n

    def load(self, i, src, eng="sync"):
        s = i % self.n
        if self.free[s] is not None:
            self.b.wait(eng, *self.free[s])
            self.free[s] = None
        tok = self.b.dma(eng, self.aps[s], src, self.snames[s])
        return self.aps[s], self.snames[s], tok

    def set_free(self, i, sem_name, tok):
        self.free[i % self.n] = (sem_name, tok)

    def slot(self, i):
        return self.aps[i % self.n]


class SpillSlots:

    def __init__(self, b, name, n):
        self.b = b
        self.snames = []
        for i in range(n):
            sn = f"{name}{i}"
            b.sem(sn)
            self.snames.append(sn)
        self.n = n

    def spill(self, i, dst, src, eng="scalar"):
        s = i % self.n
        tok = self.b.dma(eng, dst, src, self.snames[s])
        return self.snames[s], tok


def build_program(debug_taps=(), phases=("P0", "P1", "P1b", "P23", "P4", "P5", "P6")):
    nc = bass.Bass("TRN2", target_bir_lowering=False, debug=False)
    stack = contextlib.ExitStack()
    b = Builder(nc, stack)

    def din(name, shape, dt=F32):
        return nc.dram_tensor(name, shape, dt, kind="ExternalInput").ap()

    x_in = din("x", [S, D])
    ln_g = din("ln_g", [D])
    ln_b = din("ln_b", [D])
    aw = {}
    for p in ("h1", "h2", "fm"):
        aw[p + "_w1"] = din(p + "_w1", [D, DI])
        aw[p + "_b1"] = din(p + "_b1", [DI])
        aw[p + "_w2"] = din(p + "_w2", [DI, D])
        aw[p + "_b2"] = din(p + "_b2", [D])
    w_gate = din("w_gate", [H, D], I32)
    w_up = din("w_up", [H, D], I32)
    w_down = din("w_down", [D, H], I32)
    y_out = nc.dram_tensor("y", [S, D], F32, kind="ExternalOutput").ap()

    def scratch(name, shape, dt):
        kind = "ExternalOutput" if name in debug_taps else "Internal"
        return nc.dram_tensor(name, shape, dt, kind=kind).ap()

    sc_xn = scratch("sc_xn", [NT, 128, D], BF16)
    sc_ntm = scratch("sc_ntm", [NT, 128, D], BF16)
    sc_w1 = scratch("sc_w1", [NC, 128, S], BF16)
    sc_w2 = scratch("sc_w2", [NC, 128, S], BF16)
    sc_x1nfm = scratch("sc_x1nfm", [NC, 128, S], BF16)
    sc_attns = scratch("sc_attns", [NT, 128, D], F32)
    cw1 = {p: scratch("cw_" + p + "w1", [NC, 128, DI], BF16) for p in ("h1", "h2", "fm")}
    cw2 = {p: scratch("cw_" + p + "w2", [8, 32, 128, 128], BF16) for p in ("h1", "h2", "fm")}
    cw_wgT = scratch("cw_wgT", [NC, 128, H], BF16)
    cw_wuT = scratch("cw_wuT", [NC, 128, H], BF16)
    cw_wdT = scratch("cw_wdT", [NH, 128, D], BF16)
    dbg_nfm = None
    if "dbg_nfm" in debug_taps:
        dbg_nfm = nc.dram_tensor("dbg_nfm", [NC, 128, S], BF16, kind="ExternalOutput").ap()
    dbg_st = None
    if "dbg_st" in debug_taps:
        dbg_st = nc.dram_tensor("dbg_st", [12, 128, NT], F32, kind="ExternalOutput").ap()

    TB1 = nc.alloc_sbuf_tensor("TB1", [128, 32768], BF16)
    TB2 = nc.alloc_sbuf_tensor("TB2", [128, 32768], BF16)
    TB3 = nc.alloc_sbuf_tensor("TB3", [128, 24576], BF16)
    TF = nc.alloc_sbuf_tensor("TF", [128, 5120], F32)
    ident_b = nc.alloc_sbuf_tensor("ident_b", [128, 128], BF16)
    ident_f = nc.alloc_sbuf_tensor("ident_f", [128, 128], F32)
    maskA = nc.alloc_sbuf_tensor("maskA", [128, 256], BF16)
    maskB = nc.alloc_sbuf_tensor("maskB", [128, 256], BF16)
    g_bc = nc.alloc_sbuf_tensor("g_bc", [128, D], BF16)
    bb_bc = nc.alloc_sbuf_tensor("bb_bc", [128, D], BF16)
    b2fm_bc = nc.alloc_sbuf_tensor("b2fm_bc", [128, D], BF16)
    bias1 = {p: nc.alloc_sbuf_tensor("bias1_" + p, [128, 32], F32) for p in ("h1", "h2", "fm")}
    bias2 = {p: nc.alloc_sbuf_tensor("bias2_" + p, [128, 8], F32) for p in ("h1", "h2", "fm")}
    st = {n: nc.alloc_sbuf_tensor("st_" + n, [128, NT], F32)
          for n in ("s1", "q1", "mu1", "nmu1", "t01", "v1", "r1", "nb1",
                    "s2", "q2", "mu2", "nmu2", "t02", "v2", "r2", "nb2")}
    st4 = {n: nc.alloc_sbuf_tensor("st4_" + n, [128, 8], F32)
           for n in ("su", "qu", "mu", "vu", "ru", "nbu",
                     "sx", "qx", "mx", "vx", "rx", "nbx")}
    stage_t = nc.alloc_sbuf_tensor("stage_t", [32, 128], F32)
    c_eps = nc.alloc_sbuf_tensor("c_eps", [128, 1], F32)
    c_invD = nc.alloc_sbuf_tensor("c_invD", [128, 1], F32)
    c_neg1 = nc.alloc_sbuf_tensor("c_neg1", [128, 1], F32)

    def tf(i):
        return TF[:, i * 1024:(i + 1) * 1024]

    PA = nc.alloc_psum_tensor("PA", [128, 1024], F32)
    PB = nc.alloc_psum_tensor("PB", [128, 1024], F32)
    PC = nc.alloc_psum_tensor("PC", [128, 1024], F32)
    PE = [nc.alloc_psum_tensor("PE1", [128, 1024], BF16),
          nc.alloc_psum_tensor("PE2", [128, 1024], BF16)]

    for s in ("L", "LG", "LB", "G", "SP", "TE", "VE", "SE", "GE"):
        b.sem(s)

    def emit_P0():
        b.op("gpsimd", lambda e: e.memset(c_eps[:], EPS))
        b.op("gpsimd", lambda e: e.memset(c_invD[:], 1.0 / D))
        b.op("gpsimd", lambda e: e.memset(c_neg1[:], -1.0))
        tm0 = b.op("gpsimd", lambda e: e.memset(ident_b[:], 0.0), "GE")
        b.wait("gpsimd", "GE", tm0)
        b.op("gpsimd", lambda e: e.affine_select(
            out=ident_b[:], in_=ident_b[:], compare_op=ALU.not_equal,
            fill=1.0, base=0, pattern=[[-1, 128]], channel_multiplier=1))
        tm1 = b.op("gpsimd", lambda e: e.memset(ident_f[:], 0.0), "GE")
        b.wait("gpsimd", "GE", tm1)
        b.op("gpsimd", lambda e: e.affine_select(
            out=ident_f[:], in_=ident_f[:], compare_op=ALU.not_equal,
            fill=1.0, base=0, pattern=[[-1, 128]], channel_multiplier=1))
        tm2 = b.op("gpsimd", lambda e: e.memset(maskA[:], 1.0), "GE")
        b.wait("gpsimd", "GE", tm2)
        b.op("gpsimd", lambda e: e.affine_select(
            out=maskA[:], in_=maskA[:], compare_op=ALU.is_ge,
            fill=0.0, base=0, pattern=[[1, 256]], channel_multiplier=-1))
        tm3 = b.op("gpsimd", lambda e: e.memset(maskB[:], 1.0), "GE")
        b.wait("gpsimd", "GE", tm3)
        t_mk = b.op("gpsimd", lambda e: e.affine_select(
            out=maskB[:], in_=maskB[:], compare_op=ALU.is_ge,
            fill=0.0, base=-128, pattern=[[1, 256]], channel_multiplier=-1), "GE")

        b.dma("sync", tf(0), ln_g[None, :].broadcast_to((128, D)), "LG")
        b.dma("sync", tf(1), ln_b[None, :].broadcast_to((128, D)), "LG")
        t_b3 = b.dma("sync", tf(2), aw["fm_b2"][None, :].broadcast_to((128, D)), "LG")
        b.wait("vector", "LG", t_b3)
        b.op("vector", lambda e: e.tensor_copy(out=g_bc[:], in_=tf(0)))
        b.op("vector", lambda e: e.tensor_copy(out=bb_bc[:], in_=tf(1)))
        t_gb = b.op("vector", lambda e: e.tensor_copy(out=b2fm_bc[:], in_=tf(2)), "VE")

        for p in ("h1", "h2", "fm"):
            for (src, dst, nch) in ((aw[p + "_b1"], bias1[p], 32), (aw[p + "_b2"], bias2[p], 8)):
                tok_l = b.dma("sync", stage_t[0:nch, :], src.rearrange("(c p) -> c p", p=128), "LB")
                b.wait("tensor", "LB", tok_l)
                b.wait("tensor", "GE", t_mk)
                tok_t = b.op("tensor", lambda e, nch=nch: e.transpose(
                    PC[:, 0:nch], stage_t[0:nch, :], ident_f[0:nch, 0:nch]), "TE")
                b.wait("vector", "TE", tok_t)
                tok_v = b.op("vector", lambda e, dst=dst, nch=nch: e.tensor_copy(
                    out=dst[:, 0:nch], in_=PC[:, 0:nch]), "VE")
                b.wait("tensor", "VE", tok_v)
                b.wait("sync", "TE", tok_t)
        return t_gb

    def p1_builder(t_gb, wc):
        xring = Ring(b, "Rx", [tf(0), tf(1), tf(2)])
        if t_gb is not None:
            for i in range(3):
                xring.set_free(i, "VE", t_gb)
            b.wait("scalar", "VE", t_gb)
        xnb = [TB3[:, i * 1024:(i + 1) * 1024] for i in range(4)]
        ntm = [TB3[:, 4096 + i * 1024:4096 + (i + 1) * 1024] for i in range(2)]
        nfm_view = TB1[:, 0:NC * S].rearrange("p (c z) -> p c z", c=NC)
        sp_xn = SpillSlots(b, "SpXn", 4)
        sp_nt = SpillSlots(b, "SpNt", 2)
        state = {"xnb_spd": {}, "xnb_rd": {}, "ntm_tpd": {}, "ntm_spd": {},
                 "pe_free": {}, "pe_group": {}, "lseq": 0}

        def batched_chain(lvl, k):
            zc = slice(4 * k, 4 * k + 4)
            s_, q_, mu_, v_, r_, nb_ = (st["s" + lvl], st["q" + lvl], st["mu" + lvl],
                                        st["v" + lvl], st["r" + lvl], st["nb" + lvl])
            b.drain("vector")
            b.wait("vector", "SE", b.cur("SE"))
            b.op("vector", lambda e: e.tensor_scalar(out=mu_[:, zc], in0=s_[:, zc],
                 scalar1=1.0 / D, scalar2=None, op0=ALU.mult))
            b.drain("vector")
            b.op("vector", lambda e: e.tensor_tensor(out=v_[:, zc], in0=mu_[:, zc],
                 in1=mu_[:, zc], op=ALU.mult))
            b.drain("vector")
            tv = b.op("vector", lambda e: e.scalar_tensor_tensor(out=v_[:, zc],
                 in0=q_[:, zc], scalar=1.0 / D, in1=v_[:, zc], op0=ALU.mult, op1=ALU.subtract))
            b.wait("scalar", "VE", tv)
            ts_ = b.op("scalar", lambda e: e.activation(out=r_[:, zc], in_=v_[:, zc],
                 func=AF.Sqrt, bias=c_eps[:]))
            b.wait("vector", "SE", ts_)
            b.op("vector", lambda e: e.reciprocal(out=r_[:, zc], in_=r_[:, zc]))
            b.drain("vector")
            tnb = b.op("vector", lambda e: e.scalar_tensor_tensor(out=nb_[:, zc],
                 in0=mu_[:, zc], scalar=-1.0, in1=r_[:, zc], op0=ALU.mult, op1=ALU.mult))
            b.drain("vector")
            b.wait("scalar", "VE", tnb)

        jobs = []

        def j_sweepA(zt):
            li = state["lseq"]
            state["lseq"] += 1
            xs, sn, tok = xring.load(li, x_in[zt * 128:(zt + 1) * 128, :])
            slot4 = zt % 4
            b.wait("scalar", sn, tok)
            tp1 = b.op("scalar", lambda e, xs=xs, zt=zt: e.activation(
                out=xs, in_=xs, func=AF.Identity,
                accum_out=st["s1"][:, zt:zt + 1]))
            b.wait("vector", "SE", tp1)
            if zt >= 4:
                b.wait("vector", *state["xnb_spd"][zt - 4])
            ca = b.op("vector", lambda e, xs=xs, zt=zt, slot4=slot4: e.scalar_tensor_tensor(
                out=xnb[slot4], in0=xs, scalar=1.0, in1=xs, op0=ALU.mult, op1=ALU.mult,
                accum_out=st["q1"][:, zt:zt + 1]))
            xring.set_free(li, "VE", ca)
            wc.pump(2, allow_ffn=False)

        def group(k):
            tiles = range(4 * k, 4 * k + 4)
            for zt in tiles:
                j_sweepA(zt)
            batched_chain("1", k)
            for zt in tiles:
                j_sweepB(zt)
            batched_chain("2", k)
            for zt in tiles:
                j_sweepC(zt)
            state["pe_group"][k] = state["pe_free"][4 * k + 3]

        def j_sweepB(zt):
            li = state["lseq"]
            state["lseq"] += 1
            xs, sn, tok = xring.load(li, x_in[zt * 128:(zt + 1) * 128, :])
            slot = zt % 4
            b.wait("vector", sn, tok)
            tsx = b.op("vector", lambda e, xs=xs, zt=zt, slot=slot: e.tensor_scalar(
                out=xnb[slot], in0=xs,
                scalar1=st["r1"][:, zt:zt + 1], scalar2=st["nb1"][:, zt:zt + 1],
                op0=ALU.mult, op1=ALU.add))
            b.wait("scalar", "VE", tsx)
            b.op("scalar", lambda e, slot=slot, zt=zt: e.activation(
                out=xnb[slot], in_=xnb[slot], func=AF.Identity,
                accum_out=st["s2"][:, zt:zt + 1]))
            b.wait("vector", "SE", b.cur("SE"))
            cb = b.op("vector", lambda e, xs=xs, slot=slot, zt=zt: e.scalar_tensor_tensor(
                out=xs, in0=xnb[slot], scalar=1.0, in1=xnb[slot],
                op0=ALU.mult, op1=ALU.mult, accum_out=st["q2"][:, zt:zt + 1]))
            xring.set_free(li, "VE", cb)
            b.wait("scalar", "VE", cb)
            state["xnb_spd"][zt] = sp_xn.spill(zt, sc_xn[zt], xnb[slot])
            wc.pump(2, allow_ffn=False)

        def j_sweepC(zt):
            if True:
                slot = zt % 2
                xc = xnb[zt % 4]
                b.op("vector", lambda e, xc=xc, zt=zt: e.tensor_scalar(
                    out=tf(3), in0=xc,
                    scalar1=st["r2"][:, zt:zt + 1], scalar2=st["nb2"][:, zt:zt + 1],
                    op0=ALU.mult, op1=ALU.add))
                b.op("vector", lambda e: e.tensor_tensor(out=tf(3), in0=tf(3), in1=g_bc[:], op=ALU.mult))
                if zt >= 2:
                    b.wait("vector", "TE", state["ntm_tpd"][zt - 2])
                    b.wait("vector", *state["ntm_spd"][zt - 2])
                tb_ = b.op("vector", lambda e, slot=slot: e.tensor_tensor(
                    out=ntm[slot], in0=tf(3), in1=bb_bc[:], op=ALU.add))
                state["xnb_rd"][zt] = tb_
                b.wait("scalar", "VE", tb_)
                state["ntm_spd"][zt] = sp_nt.spill(zt, sc_ntm[zt], ntm[slot])
                b.wait("tensor", "VE", tb_)
                if zt >= 2:
                    b.wait("tensor", "VE", state["pe_free"][zt - 2])
                tok_tp = None
                for j in range(NC):
                    tok_tp = b.op("tensor", lambda e, j=j, slot=slot: e.transpose(
                        PE[slot][:, j * 128:(j + 1) * 128], ntm[slot][:, j * 128:(j + 1) * 128],
                        ident_b[:]))
                state["ntm_tpd"][zt] = tok_tp
                b.wait("vector", "TE", tok_tp)
                state["pe_free"][zt] = b.op("vector", lambda e, slot=slot, zt=zt: e.tensor_copy(
                    out=nfm_view[:, :, zt * 128:(zt + 1) * 128],
                    in_=PE[slot][:].rearrange("p (c z) -> p c z", c=NC)))
                wc.pump(2, allow_ffn=False)

        def queue_group(k):
            tiles = list(range(4 * k, 4 * k + 4))
            for zt in tiles:
                jobs.append(lambda zt=zt: j_sweepA(zt))
            jobs.append(lambda k=k: batched_chain("1", k))
            for zt in tiles:
                jobs.append(lambda zt=zt: j_sweepB(zt))
            jobs.append(lambda k=k: batched_chain("2", k))
            for zt in tiles:
                jobs.append(lambda zt=zt: j_sweepC(zt))
            jobs.append(lambda k=k: state["pe_group"].__setitem__(k, state["pe_free"][4 * k + 3]))

        def p1_pump(n):
            c = 0
            while jobs and c < n:
                jobs.pop(0)()
                c += 1

        return group, queue_group, p1_pump, state

    class WConv:

        def __init__(self):
            self.wf32 = [TB3[:, 14336 + i * 2048:14336 + (i + 1) * 2048].bitcast(F32)
                         for i in range(4)]
            self.ob2 = [TB3[:, 22528 + i * 1024:22528 + (i + 1) * 1024] for i in range(2)]
            self.fi32 = [TB3[:, 6144 + i * 2048:6144 + (i + 1) * 2048].bitcast(I32)
                         for i in range(2)]
            self.fbf = [TB3[:, 10240 + i * 1024:10240 + (i + 1) * 1024] for i in range(2)]
            self.ftr = [TB3[:, 12288 + i * 1024:12288 + (i + 1) * 1024] for i in range(2)]
            self.ld = Ring(b, "RWc", self.wf32)
            self.fld = Ring(b, "RFc", self.fi32)
            self.sp_w = SpillSlots(b, "SpWc", 2)
            self.sp_f = SpillSlots(b, "SpFc", 2)
            self.gates = {}
            self.last_sp = {}
            self.jobs = []
            self.aidx = 0
            self.fidx = 0
            self.wconv_done = {}
            self.fsp_done = {}
            self.ftr_done = {}
            self.pe_evac = {}
            self.pe_seed = None
            for p in ("h1", "h2", "fm"):
                for c in range(NC):
                    for q in range(4):
                        self.jobs.append((p + "_w1",
                                          aw[p + "_w1"][c * 128:(c + 1) * 128, q * 1024:(q + 1) * 1024],
                                          cw1[p][c][:, q * 1024:(q + 1) * 1024], False))
                for i in range(32):
                    self.jobs.append((p + "_w2", aw[p + "_w2"][i * 128:(i + 1) * 128, :],
                                      ("jsliced", cw2[p][:, i].transpose([1, 0, 2])), False))
            for t in range(NH):
                self.jobs.append(("ffn", w_gate[t * 128:(t + 1) * 128, :],
                                  cw_wgT[:, :, t * 128:(t + 1) * 128].transpose([1, 0, 2]), True))
            for t in range(NH):
                self.jobs.append(("ffn", w_up[t * 128:(t + 1) * 128, :],
                                  cw_wuT[:, :, t * 128:(t + 1) * 128].transpose([1, 0, 2]), True))
            for t in range(NC):
                for c3 in range(3):
                    self.jobs.append(("ffn", w_down[t * 128:(t + 1) * 128, c3 * 1024:(c3 + 1) * 1024],
                                      cw_wdT[c3 * 8:(c3 + 1) * 8, :, t * 128:(t + 1) * 128].transpose([1, 0, 2]), True))
            self.pos = 0

        def _attn_job(self, tag, srcap, dstap):
            idx = self.aidx
            self.aidx += 1
            buf, sn, tl = self.ld.load(idx, srcap, eng="gpsimd")
            slot = idx % 2
            ob = self.ob2[slot]
            buf1 = buf[:, 0:1024]
            b.wait("vector", sn, tl)
            if idx >= 2:
                b.wait("vector", *self.wconv_done[idx - 2])
            tv = b.op("vector", lambda e, buf1=buf1, ob=ob: e.tensor_copy(out=ob, in_=buf1))
            self.ld.set_free(idx, "VE", tv)
            b.wait("scalar", "VE", tv)
            if isinstance(dstap, tuple):
                sp = self.sp_w.spill(idx, dstap[1],
                                     ob.rearrange("p (j c) -> p j c", j=8))
            else:
                sp = self.sp_w.spill(idx, dstap, ob)
            self.wconv_done[idx] = sp
            self.last_sp.setdefault(tag, []).append(sp)

        def _ffn_job(self, tag, srcap, dstap):
            k = self.fidx
            self.fidx += 1
            buf, sn, tl = self.fld.load(k, srcap, eng="gpsimd")
            slot = k % 2
            b.wait("vector", sn, tl)
            if k >= 2:
                b.wait("vector", "TE", self.ftr_done[k - 2])
            tc = b.op("vector", lambda e, buf=buf, o=self.fbf[slot]: e.tensor_scalar(
                out=o, in0=buf, scalar1=1.0 / 127.0, scalar2=None, op0=ALU.mult))
            self.fld.set_free(k, "VE", tc)
            b.wait("tensor", "VE", tc)
            if k >= 2:
                b.wait("tensor", "VE", self.pe_evac[k - 2])
            elif self.pe_seed is not None:
                b.wait("tensor", "VE", self.pe_seed)
            tp = None
            for j in range(NC):
                tp = b.op("tensor", lambda e, j=j, slot=slot: e.transpose(
                    PE[slot][:, j * 128:(j + 1) * 128],
                    self.fbf[slot][:, j * 128:(j + 1) * 128], ident_b[:]))
            self.ftr_done[k] = tp
            b.wait("vector", "TE", tp)
            if k >= 2:
                b.wait("vector", *self.fsp_done[k - 2])
            tv = b.op("vector", lambda e, slot=slot: e.tensor_copy(
                out=self.ftr[slot].rearrange("p (j c) -> p j c", j=8),
                in_=PE[slot][:].rearrange("p (j c) -> p j c", j=8)))
            self.pe_evac[k] = tv
            b.wait("scalar", "VE", tv)
            sp = self.sp_f.spill(k, dstap, self.ftr[slot].rearrange("p (j c) -> p j c", j=8))
            self.fsp_done[k] = sp
            self.last_sp.setdefault("ffn", []).append(sp)

        def pump(self, n, allow_ffn=True):
            done = 0
            while done < n and self.pos < len(self.jobs):
                tag, srcap, dstap, is_ffn = self.jobs[self.pos]
                if is_ffn and not allow_ffn:
                    return done
                if is_ffn and self.pe_seed is None:
                    self.pe_seed = b.cur("VE")
                self.pos += 1
                if is_ffn:
                    self._ffn_job(tag, srcap, dstap)
                else:
                    self._attn_job(tag, srcap, dstap)
                done += 1
            return done

        def gate(self, tag):
            if tag not in self.gates:
                while self.pos < len(self.jobs) and any(
                        j[0] == tag for j in self.jobs[self.pos:]):
                    self.pump(1)
                for sp in self.last_sp[tag][-2:]:
                    b.wait("scalar", *sp)
                self.gates[tag] = b.op("scalar", lambda e: e.nop())
            return self.gates[tag]

    def mlp_builder(p, wc, epilogue, rhs_wait=None, pump=None):
        rhs_view = TB1[:, 0:NC * S].rearrange("p (c z) -> p c z", c=NC)
        w1s = [TB2[:, 16384 + i * 4096:16384 + (i + 1) * 4096].rearrange("p (c w) -> p c w", c=NC)
               for i in range(2)]
        w2s = [TB2[:, 24576 + i * 4096:24576 + (i + 1) * 4096].rearrange("p (i c) -> p i c", i=32)
               for i in range(2)]
        h_sb = TB2[:, 0:16384]
        r_w1 = Ring(b, "Rw1" + p, w1s)
        r_w2 = Ring(b, "Rw2" + p, w2s)
        g1 = wc.gate(p + "_w1")
        g2 = wc.gate(p + "_w2")
        b.wait("sync", "SE", g1)
        b.wait("sync", "SE", g2)
        PH = [PA[:, 0:512], PA[:, 512:1024], PB[:, 0:512], PB[:, 512:1024]]
        PO = [PC[:, 0:512], PC[:, 512:1024]]
        mst = {"ph_free": {}, "po_free": {}, "hseq": 0, "oseq": 0, "hsb_war": None}

        def slice_fn(zs):
            ph_free, po_free = mst["ph_free"], mst["po_free"]
            hseq, oseq = mst["hseq"], mst["oseq"]
            if rhs_wait is not None:
                rhs_wait(zs)
            last_evac = None
            for q in range(8):
                w1buf, sn1, t1 = r_w1.load(zs * 8 + q, cw1[p][:, :, q * 512:(q + 1) * 512].transpose([1, 0, 2]))
                b.wait("tensor", sn1, t1)
                for dl in range(4):
                    di = q * 4 + dl
                    rot = hseq % 4
                    hseq += 1
                    if rot in ph_free:
                        b.wait("tensor", "SE", ph_free[rot])
                    pt = None
                    for c in range(NC):
                        pt = b.op("tensor", lambda e, c=c, dl=dl, w1buf=w1buf, rot=rot, zs=zs: e.matmul(
                            PH[rot], w1buf[:, c, dl * 128:(dl + 1) * 128],
                            rhs_view[:, c, zs * 512:(zs + 1) * 512],
                            start=(c == 0), stop=(c == NC - 1)))
                    r_w1.set_free(zs * 8 + q, "TE", pt)
                    if pump is not None:
                        pump(1)
                    b.wait("scalar", "TE", pt)
                    if mst["hsb_war"] is not None:
                        b.wait("scalar", "TE", mst["hsb_war"])
                    last_evac = b.op("scalar", lambda e, rot=rot, di=di: e.activation(
                        out=h_sb[:, di * 512:(di + 1) * 512], in_=PH[rot], func=AF.Gelu,
                        bias=bias1[p][:, di:di + 1]))
                    ph_free[rot] = last_evac
            pb_last = None
            for j in range(NC):
                w2buf, sn2, t2 = r_w2.load(zs * 8 + j, cw2[p][j].transpose([1, 0, 2]))
                b.wait("tensor", sn2, t2)
                rot = oseq % 2
                oseq += 1
                if rot in po_free:
                    b.wait("tensor", "SE", po_free[rot])
                b.wait("tensor", "SE", last_evac)
                pt = None
                for i in range(32):
                    pt = b.op("tensor", lambda e, i=i, w2buf=w2buf, rot=rot: e.matmul(
                        PO[rot], w2buf[:, i, :], h_sb[:, i * 512:(i + 1) * 512],
                        start=(i == 0), stop=(i == 31)))
                r_w2.set_free(zs * 8 + j, "TE", pt)
                pb_last = pt
                if pump is not None:
                    pump(1)
                tok_ep = epilogue(j, zs, PO[rot], pt)
                po_free[rot] = tok_ep
            mst["hseq"], mst["oseq"], mst["hsb_war"] = hseq, oseq, pb_last

        return slice_fn, mst

    def emit_P23(wc, p1_group, p1_queue, p1_pump, p1_state):
        outbuf = [tf(4).bitcast(BF16)[:, 0:512], tf(4).bitcast(BF16)[:, 512:1024]]
        sp_o = SpillSlots(b, "SpW" + "x", 2)
        state = {"n": 0, "free": {}}

        def mk_epilogue(dst):
            def epilogue(j, zs, psum, pt):
                k = state["n"]
                state["n"] += 1
                if k == 0:
                    b.wait("scalar", "VE", b.cur("VE"))
                slot = k % 2
                b.wait("scalar", "TE", pt)
                if slot in state["free"]:
                    b.wait("scalar", *state["free"][slot])
                bb = bias2[cur_p[0]][:, j:j + 1]
                te = b.op("scalar", lambda e, psum=psum, slot=slot, bb=bb: e.activation(
                    out=outbuf[slot], in_=psum, func=AF.Identity, bias=bb))
                b.wait("scalar", "SE", te)
                sp = sp_o.spill(k, dst[j][:, zs * 512:(zs + 1) * 512], outbuf[slot])
                state["free"][slot] = sp
                return te
            return epilogue

        cur_p = ["h1"]

        def rhs_wait_h1(zs):
            b.wait("tensor", "VE", p1_state["pe_group"][zs])

        h1_slice, h1_mst = mlp_builder("h1", wc, mk_epilogue(sc_w1), rhs_wait_h1,
                                       pump=p1_pump)
        p1_group(0)
        p1_group(1)
        for k in range(S // 512):
            if k + 2 < S // 512:
                p1_queue(k + 2)
            h1_slice(k)
            p1_pump(10 ** 9)
        wc.pump(10 ** 9)
        cur_p[0] = "h2"
        h2_slice, h2_mst = mlp_builder("h2", wc, mk_epilogue(sc_w2))
        for k in range(S // 512):
            h2_slice(k)
        return h2_mst["hsb_war"]

    def emit_P4(p23_done):
        b.sem("PL4")
        b.wait("gpsimd", "TE", p23_done)
        for sn in ("SpWx0", "SpWx1", "SpNt0", "SpNt1", "SpXn0", "SpXn1", "SpXn2", "SpXn3"):
            b.wait("gpsimd", sn, b.cur(sn))
        w2v = TB1[:, 0:NC * S].rearrange("p (c z) -> p c z", c=NC)
        pl_toks = []
        for c in range(NC):
            pl_toks.append(b.dma("gpsimd", w2v[:, c, :], sc_w2[c], "PL4"))
        ntm_v = TB2[:, 0:NT * D].rearrange("p (t f) -> p t f", t=NT)
        for t in range(NT):
            pl_toks.append(b.dma("gpsimd", ntm_v[:, t, :], sc_ntm[t], "PL4"))
        pl_tok = pl_toks[-1]

        w1slab = [TB3[:, 0:2048].rearrange("p (c z) -> p c z", c=NC),
                  TB3[:, 2048:4096].rearrange("p (c z) -> p c z", c=NC)]
        w3sb = [TB3[:, 4096 + i * 256:4096 + (i + 1) * 256] for i in range(3)]
        xnr = [TB3[:, 4864 + i * 1024:4864 + (i + 1) * 1024] for i in range(2)]
        uring = [TB3[:, 6912 + i * 1024:6912 + (i + 1) * 1024] for i in range(8)]
        x1ring = [TB3[:, 15104 + i * 1024:15104 + (i + 1) * 1024] for i in range(8)]
        r_w1s = Ring(b, "Rw1s", [w1slab[0], w1slab[1]])
        r_xn = Ring(b, "Rxn4", xnr)
        sp_x1 = SpillSlots(b, "SpX1", 2)
        x1nfm_sp = {}

        PT = [PA, PB]
        PW3 = [PC[:, 0:512], PC[:, 512:1024]]
        w3_ev = {}
        w3_rd = {}
        pt_ev = {}
        pe_ev = {}
        u_rd = {}
        x1_tp = {}
        NB = 4
        b.wait("tensor", "VE", b.cur("VE"))
        b.wait("sync", "TE", p23_done)
        for sn in ("SpXn0", "SpXn1", "SpXn2", "SpXn3", "SpNt0", "SpNt1",
                   "SpWc0", "SpWc1", "SpFc0", "SpFc1", "SpWx0", "SpWx1"):
            b.wait("sync", sn, b.cur(sn))
        b.wait("sync", "VE", b.cur("VE"))
        b.wait("sync", "GE", b.cur("GE"))
        b.wait("vector", "SpWx0", b.cur("SpWx0"))
        b.wait("vector", "SpWx1", b.cur("SpWx1"))

        def uchain_job(base):
            b.wait("vector", "SE", b.cur("SE"))
            b.drain("vector")
            b.op("vector", lambda e: e.tensor_scalar(out=st4["mu"][:], in0=st4["su"][:],
                 scalar1=1.0 / D, scalar2=None, op0=ALU.mult))
            b.drain("vector")
            b.op("vector", lambda e: e.tensor_tensor(out=st4["vu"][:], in0=st4["mu"][:],
                 in1=st4["mu"][:], op=ALU.mult))
            b.drain("vector")
            tvu = b.op("vector", lambda e: e.scalar_tensor_tensor(out=st4["vu"][:],
                 in0=st4["qu"][:], scalar=1.0 / D, in1=st4["vu"][:],
                 op0=ALU.mult, op1=ALU.subtract))
            b.wait("scalar", "VE", tvu)
            tsd = b.op("scalar", lambda e: e.activation(out=st4["ru"][:], in_=st4["vu"][:],
                 func=AF.Sqrt, bias=c_eps[:]))
            b.wait("vector", "SE", tsd)
            b.op("vector", lambda e: e.reciprocal(out=st4["ru"][:], in_=st4["ru"][:]))
            b.drain("vector")
            tnb = b.op("vector", lambda e: e.scalar_tensor_tensor(out=st4["nbu"][:],
                 in0=st4["mu"][:], scalar=-1.0, in1=st4["ru"][:],
                 op0=ALU.mult, op1=ALU.mult))
            b.wait("scalar", "VE", tnb)

        def x1_job(base, i):
            g128 = base + i
            xnb_, snx, tlx = r_xn.load(g128, sc_xn[g128], eng="gpsimd")
            u_rd[i] = tno = b.op("scalar", lambda e, i=i: e.activation(
                out=tf(0), in_=uring[i], func=AF.Identity,
                bias=st4["nbu"][:, i:i + 1], scale=st4["ru"][:, i:i + 1]))
            b.wait("vector", "SE", tno)
            b.op("vector", lambda e: e.tensor_tensor(out=tf(1), in0=tf(0), in1=g_bc[:], op=ALU.mult))
            b.op("vector", lambda e: e.tensor_tensor(out=tf(1), in0=tf(1), in1=bb_bc[:], op=ALU.add))
            b.wait("vector", snx, tlx)
            if i in x1_tp:
                b.wait("vector", "TE", x1_tp[i])
            tx1 = b.op("vector", lambda e, xnb_=xnb_, i=i: e.scalar_tensor_tensor(
                out=x1ring[i], in0=tf(1), scalar=1.0, in1=xnb_,
                op0=ALU.mult, op1=ALU.add, accum_out=st4["sx"][:, i:i + 1]))
            r_xn.set_free(g128, "VE", tx1)
            b.wait("scalar", "VE", tx1)
            b.op("scalar", lambda e, i=i: e.activation(
                out=tf(4), in_=x1ring[i], func=AF.Square,
                accum_out=st4["qx"][:, i:i + 1]))

        def x1chain_job(base):
            b.wait("vector", "SE", b.cur("SE"))
            b.drain("vector")
            b.op("vector", lambda e: e.tensor_scalar(out=st4["mx"][:], in0=st4["sx"][:],
                 scalar1=1.0 / D, scalar2=None, op0=ALU.mult))
            b.drain("vector")
            b.op("vector", lambda e: e.tensor_tensor(out=st4["vx"][:], in0=st4["mx"][:],
                 in1=st4["mx"][:], op=ALU.mult))
            b.drain("vector")
            tvx = b.op("vector", lambda e: e.scalar_tensor_tensor(out=st4["vx"][:],
                 in0=st4["qx"][:], scalar=1.0 / D, in1=st4["vx"][:],
                 op0=ALU.mult, op1=ALU.subtract))
            b.wait("scalar", "VE", tvx)
            tsdx = b.op("scalar", lambda e: e.activation(out=st4["rx"][:], in_=st4["vx"][:],
                 func=AF.Sqrt, bias=c_eps[:]))
            b.wait("vector", "SE", tsdx)
            b.op("vector", lambda e: e.reciprocal(out=st4["rx"][:], in_=st4["rx"][:]))
            b.drain("vector")
            tnbx = b.op("vector", lambda e: e.scalar_tensor_tensor(out=st4["nbx"][:],
                 in0=st4["mx"][:], scalar=-1.0, in1=st4["rx"][:],
                 op0=ALU.mult, op1=ALU.mult))
            b.wait("scalar", "VE", tnbx)

        def x1n_job(base, i):
            g128 = base + i
            tno = b.op("scalar", lambda e, i=i: e.activation(
                out=tf(0), in_=x1ring[i], func=AF.Identity,
                bias=st4["nbx"][:, i:i + 1], scale=st4["rx"][:, i:i + 1]))
            b.wait("vector", "SE", tno)
            b.op("vector", lambda e: e.tensor_tensor(out=tf(1), in0=tf(0), in1=g_bc[:], op=ALU.mult))
            tx1n = b.op("vector", lambda e, i=i: e.tensor_tensor(
                out=x1ring[i], in0=tf(1), in1=bb_bc[:], op=ALU.add))
            b.wait("tensor", "VE", tx1n)
            pes = i % 2
            if pes in pe_ev:
                b.wait("tensor", "VE", pe_ev[pes])
            ttp = None
            for j in range(NC):
                ttp = b.op("tensor", lambda e, j=j, pes=pes, i=i: e.transpose(
                    PE[pes][:, j * 128:(j + 1) * 128],
                    x1ring[i][:, j * 128:(j + 1) * 128], ident_b[:]))
            x1_tp[i] = ttp
            b.wait("vector", "TE", ttp)
            pe_ev[pes] = b.op("vector", lambda e, pes=pes, i=i: e.tensor_copy(
                out=uring[i].rearrange("p (c z) -> p c z", c=NC),
                in_=PE[pes][:].rearrange("p (c z) -> p c z", c=NC)))
            b.wait("scalar", "VE", pe_ev[pes])
            x1nfm_sp[i] = sp_x1.spill(
                g128, sc_x1nfm[:, :, g128 * 128:(g128 + 1) * 128].transpose([1, 0, 2]),
                uring[i].rearrange("p (c z) -> p c z", c=NC))

        pending = []

        def pump_jobs(n):
            cnt = 0
            while pending and cnt < n:
                pending.pop(0)()
                cnt += 1

        w3seq = 0
        for zt4 in range(S // 256):
            slab, sns, tls = r_w1s.load(zt4, sc_w1[:, :, zt4 * 256:(zt4 + 1) * 256].transpose([1, 0, 2]))
            b.wait("tensor", sns, tls)
            b.wait("tensor", "PL4", pl_tok)
            nsb = 2 * zt4 + 2
            w3q = []

            def emit_w3(sb):
                rot = w3seq_l[0] % 2
                w3seq_l[0] += 1
                if rot in w3_ev:
                    b.wait("tensor", "VE", w3_ev[rot])
                tw = None
                for c in range(NC):
                    tw = b.op("tensor", lambda e, c=c, sb=sb, rot=rot, slab=slab: e.matmul(
                        PW3[rot][:, 0:256], w2v[:, c, sb * 128:(sb + 1) * 128],
                        slab[:, c, :], start=(c == 0), stop=(c == NC - 1)))
                b.wait("vector", "TE", tw)
                slot = w3seq_l[0] % 3
                if slot in w3_rd:
                    b.wait("vector", "TE", w3_rd[slot])
                if sb == 2 * zt4:
                    mask = maskA
                elif sb == 2 * zt4 + 1:
                    mask = maskB
                else:
                    mask = None
                if mask is not None:
                    w3_ev[rot] = b.op("vector", lambda e, rot=rot, slot=slot, mask=mask: e.tensor_tensor(
                        out=w3sb[slot], in0=PW3[rot][:, 0:256], in1=mask[:], op=ALU.mult))
                else:
                    w3_ev[rot] = b.op("vector", lambda e, rot=rot, slot=slot: e.tensor_copy(
                        out=w3sb[slot], in_=PW3[rot][:, 0:256]))
                return (sb, rot, slot, w3_ev[rot])

            def emit_t(sb, slot, evtok):
                b.wait("tensor", "VE", evtok)
                if sb == 0:
                    for zsub in range(2):
                        if zsub in pt_ev:
                            b.wait("tensor", "SE", pt_ev[zsub])
                tt = None
                stop_ = (sb == nsb - 1)
                for zsub in range(2):
                    for hh in range(2):
                        tt = b.op("tensor", lambda e, zsub=zsub, hh=hh, sb=sb, slot=slot, stop_=stop_: e.matmul(
                            PT[zsub][:, hh * 512:(hh + 1) * 512],
                            w3sb[slot][:, zsub * 128:(zsub + 1) * 128],
                            ntm_v[:, sb, hh * 512:(hh + 1) * 512],
                            start=(sb == 0), stop=stop_,
                            skip_group_check=True))
                w3_rd[slot] = tt
                return tt

            w3seq_l = [w3seq]
            from collections import deque
            pend = deque()
            pend.append(emit_w3(0))
            if nsb > 1:
                pend.append(emit_w3(1))
            for sb in range(2, nsb):
                nxt = emit_w3(sb)
                p0 = pend.popleft()
                tt = emit_t(p0[0], p0[2], p0[3])
                pump_jobs(2)
                pend.append(nxt)
            while pend:
                p0 = pend.popleft()
                tt = emit_t(p0[0], p0[2], p0[3])
                pump_jobs(1)
            w3seq = w3seq_l[0]
            r_w1s.set_free(zt4, "TE", tt)
            for zsub in range(2):
                i128 = 2 * zt4 + zsub
                col = i128 % 8
                b.wait("scalar", "TE", tt)
                if col in u_rd:
                    b.wait("scalar", "SE", u_rd[col])
                if col in x1nfm_sp:
                    b.wait("scalar", *x1nfm_sp[col])
                pt_ev[zsub] = b.op("scalar", lambda e, zsub=zsub, col=col: e.activation(
                    out=uring[col], in_=PT[zsub][:], func=AF.Gelu,
                    accum_out=st4["su"][:, col:col + 1]))
                b.op("scalar", lambda e, col=col: e.activation(
                    out=tf(4), in_=uring[col], func=AF.Square,
                    accum_out=st4["qu"][:, col:col + 1]))
            if zt4 % NB == NB - 1:
                base = (zt4 // NB) * 8
                assert not pending
                pending.append(lambda base=base: uchain_job(base))
                for i in range(8):
                    pending.append(lambda base=base, i=i: x1_job(base, i))
                pending.append(lambda base=base: x1chain_job(base))
                for i in range(8):
                    pending.append(lambda base=base, i=i: x1n_job(base, i))
        pump_jobs(10 ** 9)
        b.wait("scalar", *x1nfm_sp[6])
        b.wait("scalar", *x1nfm_sp[7])
        return b.op("scalar", lambda e: e.nop())

    def emit_P5(wc, p4_done):
        b.sem("PL5")
        b.wait("gpsimd", "SE", p4_done)
        b.wait("gpsimd", "TE", b.cur("TE"))
        for sn in ("SpX10", "SpX11"):
            b.wait("gpsimd", sn, b.cur(sn))
        rhs_view = TB1[:, 0:NC * S].rearrange("p (c z) -> p c z", c=NC)
        pl = None
        for c in range(NC):
            pl = b.dma("gpsimd", rhs_view[:, c, :], sc_x1nfm[c], "PL5")
        fmv = TB3[:, 0:4096].rearrange("p (j z) -> p j z", j=NC)
        r_x = Ring(b, "Rx5", [tf(2), tf(3)])
        r_xn5 = Ring(b, "Rxn5", [TB3[:, 4096 + i * 1024:4096 + (i + 1) * 1024] for i in range(2)])
        sp_at = SpillSlots(b, "SpAt", 2)
        b.wait("scalar", "VE", b.cur("VE"))
        b.wait("scalar", "TE", b.cur("TE"))
        for sn in ("SpX10", "SpX11", "SpWx0", "SpWx1"):
            b.wait("scalar", sn, b.cur(sn))
        for i in range(2):
            r_x.set_free(i, "SE", b.cur("SE"))
            r_xn5.set_free(i, "SE", b.cur("SE"))
        st5 = {"tp_last": None, "pe_ev": {}, "at_sp": {}, "ep_done": {}}
        b.wait("tensor", "PL5", pl)

        def ep5(j, zs, psum, pt):
            b.wait("scalar", "TE", pt)
            if st5["tp_last"] is not None:
                b.wait("scalar", "TE", st5["tp_last"])
            te = b.op("scalar", lambda e, psum=psum, j=j: e.activation(
                out=fmv[:, j, :], in_=psum, func=AF.Identity,
                bias=bias2["fm"][:, j:j + 1]))
            st5["ep_done"][j] = te
            if j == NC - 1:
                for zloc in range(4):
                    g128 = zs * 4 + zloc
                    xb, snx, tlx = r_x.load(g128, x_in[g128 * 128:(g128 + 1) * 128, :], eng="sync")
                    xnb_, snn, tln = r_xn5.load(g128, sc_xn[g128], eng="gpsimd")
                    pes = zloc % 2
                    b.wait("tensor", "SE", te)
                    if pes in st5["pe_ev"]:
                        b.wait("tensor", "VE", st5["pe_ev"][pes])
                    ttp = None
                    for jj in range(NC):
                        ttp = b.op("tensor", lambda e, jj=jj, zloc=zloc, pes=pes: e.transpose(
                            PE[pes][:, jj * 128:(jj + 1) * 128],
                            fmv[:, jj, zloc * 128:(zloc + 1) * 128], ident_b[:]))
                    st5["tp_last"] = ttp
                    outsl = [tf(0), tf(1)][g128 % 2]
                    b.wait("vector", "TE", ttp)
                    b.wait("vector", snn, tln)
                    b.wait("vector", snx, tlx)
                    if g128 >= 2:
                        b.wait("vector", *st5["at_sp"][g128 - 2])
                    tv1 = b.op("vector", lambda e, pes=pes, xnb_=xnb_: e.scalar_tensor_tensor(
                        out=tf(4), in0=PE[pes][:], scalar=1.0, in1=xnb_,
                        op0=ALU.mult, op1=ALU.add))
                    st5["pe_ev"][pes] = tv1
                    r_xn5.set_free(g128, "VE", tv1)
                    tv2 = b.op("vector", lambda e, xb=xb, outsl=outsl: e.tensor_tensor(
                        out=outsl, in0=tf(4), in1=xb, op=ALU.add))
                    r_x.set_free(g128, "VE", tv2)
                    b.wait("scalar", "VE", tv2)
                    st5["at_sp"][g128] = sp_at.spill(g128, sc_attns[g128], outsl)
            return te

        fm_slice, fm_mst = mlp_builder("fm", wc, ep5)
        for k in range(S // 512):
            fm_slice(k)
        b.wait("scalar", *st5["at_sp"][S // 128 - 1])
        b.wait("scalar", *st5["at_sp"][S // 128 - 2])
        return b.op("scalar", lambda e: e.nop())

    def emit_P6(wc, p5_done):
        ZS6 = 256
        b.sem("PL6")
        b.wait("gpsimd", "SE", wc.gate("ffn"))
        for sn in ("SpX10", "SpX11"):
            b.wait("gpsimd", sn, b.cur(sn))
        b.wait("gpsimd", "TE", b.cur("TE"))
        b.wait("gpsimd", "VE", b.cur("VE"))
        wgv = TB3[:, 0:NC * H].rearrange("p (c h) -> p c h", c=NC)
        pl6 = []
        for c in range(NC):
            pl6.append(b.dma("gpsimd", wgv[:, c, :], cw_wgT[c], "PL6"))
        b.wait("gpsimd", "SE", p5_done)
        b.wait("gpsimd", "TE", b.cur("TE"))
        wuv = TB2[:, 0:NC * H].rearrange("p (c h) -> p c h", c=NC)
        for c in range(NC):
            pl6.append(b.dma("gpsimd", wuv[:, c, :], cw_wuT[c], "PL6"))
        wdv = TB1[:, 0:NH * D].rearrange("p (i d) -> p i d", i=NH)
        for i in range(NH):
            pl6.append(b.dma("gpsimd", wdv[:, i, :], cw_wdT[i], "PL6"))
        pl_tok = pl6[-1]
        h_sb = TB2[:, 24576:24576 + NH * ZS6]
        sg = [TB2[:, 30720:30720 + ZS6], TB2[:, 30720 + ZS6:30720 + 2 * ZS6]]
        xnfm_sl = TB1[:, 24576:24576 + NC * ZS6].rearrange("p (c z) -> p c z", c=NC)
        r_xn6 = Ring(b, "Rxn6", [TB1[:, 26624 + i * 1024:26624 + (i + 1) * 1024] for i in range(2)])
        r_at = Ring(b, "Rat6", [tf(2), tf(3)])
        for i in range(2):
            r_xn6.set_free(i, "TE", b.cur("TE"))
            r_at.set_free(i, "VE", b.cur("VE"))
        sp_y = SpillSlots(b, "SpY", 2)
        b.wait("tensor", "PL6", pl_tok)
        b.wait("tensor", "VE", b.cur("VE"))
        pe_ev = {}
        sg_rd = {}
        hsb_rd = None
        pg_ev = {}
        pu_ev = {}
        pc_ev = None
        y_sp = {}
        gseq = 0
        prev_xnfm_rd = None
        for zs in range(S // ZS6):
            tpl = None
            for tz in range(ZS6 // 128):
                g32 = zs * (ZS6 // 128) + tz
                xnb_, snn, tln = r_xn6.load(g32, sc_xn[g32], eng="gpsimd")
                if zs > 0 and tz == 0:
                    b.wait("vector", "TE", prev_xnfm_rd)
                pes = gseq % 2
                gseq += 1
                b.wait("tensor", snn, tln)
                if pes in pe_ev:
                    b.wait("tensor", "VE", pe_ev[pes])
                for j in range(NC):
                    tpl = b.op("tensor", lambda e, j=j, pes=pes, xnb_=xnb_: e.transpose(
                        PE[pes][:, j * 128:(j + 1) * 128],
                        xnb_[:, j * 128:(j + 1) * 128], ident_b[:]))
                r_xn6.set_free(g32, "TE", tpl)
                b.wait("vector", "TE", tpl)
                pe_ev[pes] = b.op("vector", lambda e, pes=pes, tz=tz: e.tensor_copy(
                    out=xnfm_sl[:, :, tz * 128:(tz + 1) * 128],
                    in_=PE[pes][:, 0:NC * 128].rearrange("p (c z) -> p c z", c=NC)))
            xnfm_ready = pe_ev[(gseq - 1) % 2]
            b.wait("tensor", "VE", xnfm_ready)
            xnfm_rd = None
            for i in range(NH):
                rotg = i % 2
                if rotg in pg_ev:
                    b.wait("tensor", "SE", pg_ev[rotg])
                tg = None
                for c in range(NC):
                    tg = b.op("tensor", lambda e, c=c, i=i, rotg=rotg: e.matmul(
                        PA[:, rotg * 512:rotg * 512 + ZS6], wgv[:, c, i * 128:(i + 1) * 128],
                        xnfm_sl[:, c, :], start=(c == 0), stop=(c == NC - 1)))
                b.wait("scalar", "TE", tg)
                if rotg in sg_rd:
                    b.wait("scalar", "VE", sg_rd[rotg])
                tsg = b.op("scalar", lambda e, rotg=rotg: e.activation(
                    out=sg[rotg], in_=PA[:, rotg * 512:rotg * 512 + ZS6], func=AF.Silu))
                pg_ev[rotg] = tsg
                if rotg in pu_ev:
                    b.wait("tensor", "VE", pu_ev[rotg])
                tu = None
                for c in range(NC):
                    tu = b.op("tensor", lambda e, c=c, i=i, rotg=rotg: e.matmul(
                        PB[:, rotg * 512:rotg * 512 + ZS6], wuv[:, c, i * 128:(i + 1) * 128],
                        xnfm_sl[:, c, :], start=(c == 0), stop=(c == NC - 1)))
                xnfm_rd = tu
                b.wait("vector", "TE", tu)
                b.wait("vector", "SE", tsg)
                if hsb_rd is not None:
                    b.wait("vector", "TE", hsb_rd)
                tm = b.op("vector", lambda e, rotg=rotg, i=i: e.tensor_tensor(
                    out=h_sb[:, i * ZS6:(i + 1) * ZS6],
                    in0=PB[:, rotg * 512:rotg * 512 + ZS6], in1=sg[rotg], op=ALU.mult))
                pu_ev[rotg] = tm
                sg_rd[rotg] = tm
                h_last = tm
            td = None
            for zloc in range(ZS6 // 128):
                g128 = zs * (ZS6 // 128) + zloc
                if pc_ev is not None:
                    b.wait("tensor", "VE", pc_ev)
                b.wait("tensor", "VE", h_last)
                for i in range(NH):
                    for hh in range(2):
                        td = b.op("tensor", lambda e, i=i, hh=hh, zloc=zloc: e.matmul(
                            PC[:, hh * 512:(hh + 1) * 512],
                            h_sb[:, i * ZS6 + zloc * 128:i * ZS6 + (zloc + 1) * 128],
                            wdv[:, i, hh * 512:(hh + 1) * 512],
                            start=(i == 0), stop=(i == NH - 1),
                            skip_group_check=True))
                atb, sna, tla = r_at.load(g128, sc_attns[g128], eng="sync")
                outsl = [tf(0), tf(1)][g128 % 2]
                b.wait("vector", "TE", td)
                b.wait("vector", sna, tla)
                if g128 >= 2:
                    b.wait("vector", *y_sp[g128 - 2])
                tvy = b.op("vector", lambda e, atb=atb, outsl=outsl: e.tensor_tensor(
                    out=outsl, in0=PC[:], in1=atb, op=ALU.add))
                r_at.set_free(g128, "VE", tvy)
                pc_ev = tvy
                b.wait("scalar", "VE", tvy)
                y_sp[g128] = sp_y.spill(g128, y_out[g128 * 128:(g128 + 1) * 128, :], outsl)
            hsb_rd = td
            prev_xnfm_rd = xnfm_rd

    t_gb = emit_P0() if "P0" in phases else None
    wc = WConv()
    p23_done = None
    if "P23" in phases:
        p1_group, p1_queue, p1_pump, p1_state = p1_builder(t_gb, wc)
        p23_done = emit_P23(wc, p1_group, p1_queue, p1_pump, p1_state)
    if "P4" in phases:
        p4_done = emit_P4(p23_done)
    if "P5" in phases:
        p5_done = emit_P5(wc, p4_done)
    if "P6" in phases:
        emit_P6(wc, p5_done)

    for sname in list(b.sems):
        if sname in ("TE", "VE", "SE", "GE"):
            continue
        b.wait("gpsimd", sname, b.cur(sname))

    with stack, nc.Block() as block:
        @block.sync
        def _(e):
            for f in b.prog["sync"]:
                f(e)

        @block.gpsimd
        def _(e):
            for f in b.prog["gpsimd"]:
                f(e)

        @block.vector
        def _(e):
            for f in b.prog["vector"]:
                f(e)

        @block.scalar
        def _(e):
            for f in b.prog["scalar"]:
                f(e)

        @block.tensor
        def _(e):
            for f in b.prog["tensor"]:
                f(e)

    return nc


_NC_CACHE = {}


def _get_program():
    if "nc" not in _NC_CACHE:
        _NC_CACHE["nc"] = build_program(debug_taps=(), phases=("P0", "P23", "P4", "P5", "P6"))
    return _NC_CACHE["nc"]


def kernel(L0_attn_x, L0_attn_ln_g, L0_attn_ln_b,
           L0_attn_h1_w1, L0_attn_h1_b1, L0_attn_h1_w2, L0_attn_h1_b2,
           L0_attn_h2_w1, L0_attn_h2_b1, L0_attn_h2_w2, L0_attn_h2_b2,
           L0_attn_fm_w1, L0_attn_fm_b1, L0_attn_fm_w2, L0_attn_fm_b2,
           L0_ffn_w_gate, L0_ffn_w_up, L0_ffn_w_down):
    from concourse.bass_utils import run_bass_kernel_spmd
    x = np.ascontiguousarray(np.asarray(L0_attn_x, dtype=np.float32))
    common = {
        "ln_g": np.ascontiguousarray(np.asarray(L0_attn_ln_g, dtype=np.float32)),
        "ln_b": np.ascontiguousarray(np.asarray(L0_attn_ln_b, dtype=np.float32)),
        "h1_w1": np.ascontiguousarray(np.asarray(L0_attn_h1_w1, dtype=np.float32)),
        "h1_b1": np.ascontiguousarray(np.asarray(L0_attn_h1_b1, dtype=np.float32)),
        "h1_w2": np.ascontiguousarray(np.asarray(L0_attn_h1_w2, dtype=np.float32)),
        "h1_b2": np.ascontiguousarray(np.asarray(L0_attn_h1_b2, dtype=np.float32)),
        "h2_w1": np.ascontiguousarray(np.asarray(L0_attn_h2_w1, dtype=np.float32)),
        "h2_b1": np.ascontiguousarray(np.asarray(L0_attn_h2_b1, dtype=np.float32)),
        "h2_w2": np.ascontiguousarray(np.asarray(L0_attn_h2_w2, dtype=np.float32)),
        "h2_b2": np.ascontiguousarray(np.asarray(L0_attn_h2_b2, dtype=np.float32)),
        "fm_w1": np.ascontiguousarray(np.asarray(L0_attn_fm_w1, dtype=np.float32)),
        "fm_b1": np.ascontiguousarray(np.asarray(L0_attn_fm_b1, dtype=np.float32)),
        "fm_w2": np.ascontiguousarray(np.asarray(L0_attn_fm_w2, dtype=np.float32)),
        "fm_b2": np.ascontiguousarray(np.asarray(L0_attn_fm_b2, dtype=np.float32)),
        "w_gate": np.ascontiguousarray(np.asarray(L0_ffn_w_gate, dtype=np.int32)),
        "w_up": np.ascontiguousarray(np.asarray(L0_ffn_w_up, dtype=np.int32)),
        "w_down": np.ascontiguousarray(np.asarray(L0_ffn_w_down, dtype=np.int32)),
    }
    in_maps = [dict(common, x=np.ascontiguousarray(x[bidx])) for bidx in range(8)]
    nc = _get_program()
    res = run_bass_kernel_spmd(nc, in_maps, core_ids=list(range(8)))
    return np.stack([r["y"] for r in res.results], axis=0)


# version 15
# speedup vs baseline: 1.0088x; 1.0088x over previous
import contextlib
import numpy as np
import concourse.bass as bass
import concourse.mybir as mybir

F32 = mybir.dt.float32
BF16 = mybir.dt.bfloat16
I32 = mybir.dt.int32
AF = mybir.ActivationFunctionType
ALU = mybir.AluOpType
AX = mybir.AxisListType

S, D, DI, H = 4096, 1024, 4096, 3072
NT = S // 128
NC = D // 128
NH = H // 128
EPS = 1e-5

ENGINES = ("sync", "gpsimd", "vector", "scalar", "tensor")


class Builder:
    def __init__(self, nc, stack):
        self.nc = nc
        self.prog = {e: [] for e in ENGINES}
        self.sems = {}
        self.cnt = {}
        self.last_wait = {}
        self.auto_chain = {"scalar": True, "vector": True, "gpsimd": True}
        self._stack = stack

    def sem(self, name):
        h = self._stack.enter_context(self.nc.semaphore(name))
        self.sems[name] = h
        self.cnt[name] = 0

    def emit(self, eng, fn):
        self.prog[eng].append(fn)

    def inc(self, name, k):
        self.cnt[name] += k
        return self.cnt[name]

    def cur(self, name):
        return self.cnt[name]

    def wait(self, eng, name, tok):
        if tok is None or tok <= 0:
            return
        key = (eng, name)
        if self.last_wait.get(key, 0) >= tok:
            return
        self.last_wait[key] = tok
        h = self.sems[name]
        self.emit(eng, lambda e, h=h, tok=tok: e.wait_ge(h, tok))

    def dma(self, eng, out_ap, in_ap, sem):
        tok = self.inc(sem, 16)
        h = self.sems[sem]
        self.emit(eng, lambda e, o=out_ap, i=in_ap, h=h: e.dma_start(out=o, in_=i).then_inc(h, 16))
        return tok

    ENG_SEM = {"scalar": "SE", "vector": "VE", "gpsimd": "GE", "tensor": "TE"}

    def op(self, eng, fn, sem=None):
        own = self.ENG_SEM.get(eng)
        if sem is not None and own is not None:
            assert sem == own, (eng, sem)
        if own is None:
            self.emit(eng, fn)
            return None
        if self.auto_chain.get(eng, False):
            prev = self.cnt[own]
            self.wait(eng, own, prev)
        tok = self.inc(own, 1)
        h = self.sems[own]
        self.emit(eng, lambda e, fn=fn, h=h: fn(e).then_inc(h, 1))
        return tok

    def drain(self, eng):
        self.emit(eng, lambda e: e.drain())


class Ring:

    def __init__(self, b, name, aps):
        self.b = b
        self.name = name
        self.aps = list(aps)
        self.n = len(self.aps)
        self.snames = []
        for i in range(self.n):
            sn = f"{name}{i}"
            b.sem(sn)
            self.snames.append(sn)
        self.free = [None] * self.n

    def load(self, i, src, eng="sync"):
        s = i % self.n
        if self.free[s] is not None:
            self.b.wait(eng, *self.free[s])
            self.free[s] = None
        tok = self.b.dma(eng, self.aps[s], src, self.snames[s])
        return self.aps[s], self.snames[s], tok

    def set_free(self, i, sem_name, tok):
        self.free[i % self.n] = (sem_name, tok)

    def slot(self, i):
        return self.aps[i % self.n]


class SpillSlots:

    def __init__(self, b, name, n):
        self.b = b
        self.snames = []
        for i in range(n):
            sn = f"{name}{i}"
            b.sem(sn)
            self.snames.append(sn)
        self.n = n

    def spill(self, i, dst, src, eng="scalar"):
        s = i % self.n
        tok = self.b.dma(eng, dst, src, self.snames[s])
        return self.snames[s], tok


def build_program(debug_taps=(), phases=("P0", "P1", "P1b", "P23", "P4", "P5", "P6")):
    nc = bass.Bass("TRN2", target_bir_lowering=False, debug=False)
    stack = contextlib.ExitStack()
    b = Builder(nc, stack)

    def din(name, shape, dt=F32):
        return nc.dram_tensor(name, shape, dt, kind="ExternalInput").ap()

    x_in = din("x", [S, D])
    ln_g = din("ln_g", [D])
    ln_b = din("ln_b", [D])
    aw = {}
    for p in ("h1", "h2", "fm"):
        aw[p + "_w1"] = din(p + "_w1", [D, DI])
        aw[p + "_b1"] = din(p + "_b1", [DI])
        aw[p + "_w2"] = din(p + "_w2", [DI, D])
        aw[p + "_b2"] = din(p + "_b2", [D])
    w_gate = din("w_gate", [H, D], I32)
    w_up = din("w_up", [H, D], I32)
    w_down = din("w_down", [D, H], I32)
    y_out = nc.dram_tensor("y", [S, D], F32, kind="ExternalOutput").ap()

    def scratch(name, shape, dt):
        kind = "ExternalOutput" if name in debug_taps else "Internal"
        return nc.dram_tensor(name, shape, dt, kind=kind).ap()

    sc_xn = scratch("sc_xn", [NT, 128, D], BF16)
    sc_ntm = scratch("sc_ntm", [NT, 128, D], BF16)
    sc_w1 = scratch("sc_w1", [NC, 128, S], BF16)
    sc_w2 = scratch("sc_w2", [NC, 128, S], BF16)
    sc_x1nfm = scratch("sc_x1nfm", [NC, 128, S], BF16)
    sc_attns = scratch("sc_attns", [NT, 128, D], F32)
    cw1 = {p: scratch("cw_" + p + "w1", [NC, 128, DI], BF16) for p in ("h1", "h2", "fm")}
    cw2 = {p: scratch("cw_" + p + "w2", [8, 32, 128, 128], BF16) for p in ("h1", "h2", "fm")}
    cw_wgT = scratch("cw_wgT", [NC, 128, H], BF16)
    cw_wuT = scratch("cw_wuT", [NC, 128, H], BF16)
    cw_wdT = scratch("cw_wdT", [NH, 128, D], BF16)
    dbg_nfm = None
    if "dbg_nfm" in debug_taps:
        dbg_nfm = nc.dram_tensor("dbg_nfm", [NC, 128, S], BF16, kind="ExternalOutput").ap()
    dbg_st = None
    if "dbg_st" in debug_taps:
        dbg_st = nc.dram_tensor("dbg_st", [12, 128, NT], F32, kind="ExternalOutput").ap()

    TB1 = nc.alloc_sbuf_tensor("TB1", [128, 32768], BF16)
    TB2 = nc.alloc_sbuf_tensor("TB2", [128, 32768], BF16)
    TB3 = nc.alloc_sbuf_tensor("TB3", [128, 24576], BF16)
    TF = nc.alloc_sbuf_tensor("TF", [128, 5120], F32)
    ident_b = nc.alloc_sbuf_tensor("ident_b", [128, 128], BF16)
    ident_f = nc.alloc_sbuf_tensor("ident_f", [128, 128], F32)
    maskA = nc.alloc_sbuf_tensor("maskA", [128, 256], BF16)
    maskB = nc.alloc_sbuf_tensor("maskB", [128, 256], BF16)
    g_bc = nc.alloc_sbuf_tensor("g_bc", [128, D], BF16)
    bb_bc = nc.alloc_sbuf_tensor("bb_bc", [128, D], BF16)
    b2fm_bc = nc.alloc_sbuf_tensor("b2fm_bc", [128, D], BF16)
    bias1 = {p: nc.alloc_sbuf_tensor("bias1_" + p, [128, 32], F32) for p in ("h1", "h2", "fm")}
    bias2 = {p: nc.alloc_sbuf_tensor("bias2_" + p, [128, 8], F32) for p in ("h1", "h2", "fm")}
    st = {n: nc.alloc_sbuf_tensor("st_" + n, [128, NT], F32)
          for n in ("s1", "q1", "mu1", "nmu1", "t01", "v1", "r1", "nb1",
                    "s2", "q2", "mu2", "nmu2", "t02", "v2", "r2", "nb2")}
    st4 = {n: nc.alloc_sbuf_tensor("st4_" + n, [128, 8], F32)
           for n in ("su", "qu", "mu", "vu", "ru", "nbu",
                     "sx", "qx", "mx", "vx", "rx", "nbx")}
    stage_t = nc.alloc_sbuf_tensor("stage_t", [32, 128], F32)
    c_eps = nc.alloc_sbuf_tensor("c_eps", [128, 1], F32)
    c_invD = nc.alloc_sbuf_tensor("c_invD", [128, 1], F32)
    c_neg1 = nc.alloc_sbuf_tensor("c_neg1", [128, 1], F32)

    def tf(i):
        return TF[:, i * 1024:(i + 1) * 1024]

    PA = nc.alloc_psum_tensor("PA", [128, 1024], F32)
    PB = nc.alloc_psum_tensor("PB", [128, 1024], F32)
    PC = nc.alloc_psum_tensor("PC", [128, 1024], F32)
    PE = [nc.alloc_psum_tensor("PE1", [128, 1024], BF16),
          nc.alloc_psum_tensor("PE2", [128, 1024], BF16)]

    for s in ("L", "LG", "LB", "G", "SP", "TE", "VE", "SE", "GE"):
        b.sem(s)

    def emit_P0():
        b.op("gpsimd", lambda e: e.memset(c_eps[:], EPS))
        b.op("gpsimd", lambda e: e.memset(c_invD[:], 1.0 / D))
        b.op("gpsimd", lambda e: e.memset(c_neg1[:], -1.0))
        tm0 = b.op("gpsimd", lambda e: e.memset(ident_b[:], 0.0), "GE")
        b.wait("gpsimd", "GE", tm0)
        b.op("gpsimd", lambda e: e.affine_select(
            out=ident_b[:], in_=ident_b[:], compare_op=ALU.not_equal,
            fill=1.0, base=0, pattern=[[-1, 128]], channel_multiplier=1))
        tm1 = b.op("gpsimd", lambda e: e.memset(ident_f[:], 0.0), "GE")
        b.wait("gpsimd", "GE", tm1)
        b.op("gpsimd", lambda e: e.affine_select(
            out=ident_f[:], in_=ident_f[:], compare_op=ALU.not_equal,
            fill=1.0, base=0, pattern=[[-1, 128]], channel_multiplier=1))
        tm2 = b.op("gpsimd", lambda e: e.memset(maskA[:], 1.0), "GE")
        b.wait("gpsimd", "GE", tm2)
        b.op("gpsimd", lambda e: e.affine_select(
            out=maskA[:], in_=maskA[:], compare_op=ALU.is_ge,
            fill=0.0, base=0, pattern=[[1, 256]], channel_multiplier=-1))
        tm3 = b.op("gpsimd", lambda e: e.memset(maskB[:], 1.0), "GE")
        b.wait("gpsimd", "GE", tm3)
        t_mk = b.op("gpsimd", lambda e: e.affine_select(
            out=maskB[:], in_=maskB[:], compare_op=ALU.is_ge,
            fill=0.0, base=-128, pattern=[[1, 256]], channel_multiplier=-1), "GE")

        b.dma("sync", tf(0), ln_g[None, :].broadcast_to((128, D)), "LG")
        b.dma("sync", tf(1), ln_b[None, :].broadcast_to((128, D)), "LG")
        t_b3 = b.dma("sync", tf(2), aw["fm_b2"][None, :].broadcast_to((128, D)), "LG")
        b.wait("vector", "LG", t_b3)
        b.op("vector", lambda e: e.tensor_copy(out=g_bc[:], in_=tf(0)))
        b.op("vector", lambda e: e.tensor_copy(out=bb_bc[:], in_=tf(1)))
        t_gb = b.op("vector", lambda e: e.tensor_copy(out=b2fm_bc[:], in_=tf(2)), "VE")

        for p in ("h1", "h2", "fm"):
            for (src, dst, nch) in ((aw[p + "_b1"], bias1[p], 32), (aw[p + "_b2"], bias2[p], 8)):
                tok_l = b.dma("sync", stage_t[0:nch, :], src.rearrange("(c p) -> c p", p=128), "LB")
                b.wait("tensor", "LB", tok_l)
                b.wait("tensor", "GE", t_mk)
                tok_t = b.op("tensor", lambda e, nch=nch: e.transpose(
                    PC[:, 0:nch], stage_t[0:nch, :], ident_f[0:nch, 0:nch]), "TE")
                b.wait("vector", "TE", tok_t)
                tok_v = b.op("vector", lambda e, dst=dst, nch=nch: e.tensor_copy(
                    out=dst[:, 0:nch], in_=PC[:, 0:nch]), "VE")
                b.wait("tensor", "VE", tok_v)
                b.wait("sync", "TE", tok_t)
        return t_gb

    def p1_builder(t_gb, wc):
        xring = Ring(b, "Rx", [tf(0), tf(1), tf(2)])
        if t_gb is not None:
            for i in range(3):
                xring.set_free(i, "VE", t_gb)
            b.wait("scalar", "VE", t_gb)
        xnb = [TB3[:, i * 1024:(i + 1) * 1024] for i in range(4)]
        ntm = [TB3[:, 4096 + i * 1024:4096 + (i + 1) * 1024] for i in range(2)]
        nfm_view = TB1[:, 0:NC * S].rearrange("p (c z) -> p c z", c=NC)
        sp_xn = SpillSlots(b, "SpXn", 4)
        sp_nt = SpillSlots(b, "SpNt", 2)
        state = {"xnb_spd": {}, "xnb_rd": {}, "ntm_tpd": {}, "ntm_spd": {},
                 "pe_free": {}, "pe_group": {}, "lseq": 0}

        def batched_chain(lvl, k):
            zc = slice(4 * k, 4 * k + 4)
            s_, q_, mu_, v_, r_, nb_ = (st["s" + lvl], st["q" + lvl], st["mu" + lvl],
                                        st["v" + lvl], st["r" + lvl], st["nb" + lvl])
            b.drain("vector")
            b.wait("vector", "SE", b.cur("SE"))
            b.op("vector", lambda e: e.tensor_scalar(out=mu_[:, zc], in0=s_[:, zc],
                 scalar1=1.0 / D, scalar2=None, op0=ALU.mult))
            b.drain("vector")
            b.op("vector", lambda e: e.tensor_tensor(out=v_[:, zc], in0=mu_[:, zc],
                 in1=mu_[:, zc], op=ALU.mult))
            b.drain("vector")
            tv = b.op("vector", lambda e: e.scalar_tensor_tensor(out=v_[:, zc],
                 in0=q_[:, zc], scalar=1.0 / D, in1=v_[:, zc], op0=ALU.mult, op1=ALU.subtract))
            b.wait("scalar", "VE", tv)
            ts_ = b.op("scalar", lambda e: e.activation(out=r_[:, zc], in_=v_[:, zc],
                 func=AF.Sqrt, bias=c_eps[:]))
            b.wait("vector", "SE", ts_)
            b.op("vector", lambda e: e.reciprocal(out=r_[:, zc], in_=r_[:, zc]))
            b.drain("vector")
            tnb = b.op("vector", lambda e: e.scalar_tensor_tensor(out=nb_[:, zc],
                 in0=mu_[:, zc], scalar=-1.0, in1=r_[:, zc], op0=ALU.mult, op1=ALU.mult))
            b.drain("vector")
            b.wait("scalar", "VE", tnb)

        jobs = []

        def j_sweepA(zt):
            li = state["lseq"]
            state["lseq"] += 1
            xs, sn, tok = xring.load(li, x_in[zt * 128:(zt + 1) * 128, :])
            slot4 = zt % 4
            b.wait("vector", sn, tok)
            if zt >= 4:
                b.wait("vector", *state["xnb_spd"][zt - 4])
            b.op("vector", lambda e, xs=xs, zt=zt, slot4=slot4: e.tensor_scalar(
                out=xnb[slot4], in0=xs, scalar1=1.0, scalar2=0.0,
                op0=ALU.mult, op1=ALU.add, accum_out=st["s1"][:, zt:zt + 1]))
            ca = b.op("vector", lambda e, xs=xs, zt=zt, slot4=slot4: e.scalar_tensor_tensor(
                out=xnb[slot4], in0=xs, scalar=1.0, in1=xs, op0=ALU.mult, op1=ALU.mult,
                accum_out=st["q1"][:, zt:zt + 1]))
            xring.set_free(li, "VE", ca)
            wc.pump(2, allow_ffn=False)

        def group(k):
            tiles = range(4 * k, 4 * k + 4)
            for zt in tiles:
                j_sweepA(zt)
            batched_chain("1", k)
            for zt in tiles:
                j_sweepB(zt)
            batched_chain("2", k)
            for zt in tiles:
                j_sweepC(zt)
            state["pe_group"][k] = state["pe_free"][4 * k + 3]

        def j_sweepB(zt):
            li = state["lseq"]
            state["lseq"] += 1
            xs, sn, tok = xring.load(li, x_in[zt * 128:(zt + 1) * 128, :])
            slot = zt % 4
            b.wait("vector", sn, tok)
            tsx = b.op("vector", lambda e, xs=xs, zt=zt, slot=slot: e.tensor_scalar(
                out=xnb[slot], in0=xs,
                scalar1=st["r1"][:, zt:zt + 1], scalar2=st["nb1"][:, zt:zt + 1],
                op0=ALU.mult, op1=ALU.add))
            b.op("vector", lambda e, xs=xs, slot=slot, zt=zt: e.tensor_scalar(
                out=xs, in0=xnb[slot], scalar1=1.0, scalar2=0.0,
                op0=ALU.mult, op1=ALU.add, accum_out=st["s2"][:, zt:zt + 1]))
            cb = b.op("vector", lambda e, xs=xs, slot=slot, zt=zt: e.scalar_tensor_tensor(
                out=xs, in0=xnb[slot], scalar=1.0, in1=xnb[slot],
                op0=ALU.mult, op1=ALU.mult, accum_out=st["q2"][:, zt:zt + 1]))
            xring.set_free(li, "VE", cb)
            b.wait("scalar", "VE", cb)
            state["xnb_spd"][zt] = sp_xn.spill(zt, sc_xn[zt], xnb[slot])
            wc.pump(2, allow_ffn=False)

        def j_sweepC(zt):
            if True:
                slot = zt % 2
                xc = xnb[zt % 4]
                b.op("vector", lambda e, xc=xc, zt=zt: e.tensor_scalar(
                    out=tf(3), in0=xc,
                    scalar1=st["r2"][:, zt:zt + 1], scalar2=st["nb2"][:, zt:zt + 1],
                    op0=ALU.mult, op1=ALU.add))
                b.op("vector", lambda e: e.tensor_tensor(out=tf(3), in0=tf(3), in1=g_bc[:], op=ALU.mult))
                if zt >= 2:
                    b.wait("vector", "TE", state["ntm_tpd"][zt - 2])
                    b.wait("vector", *state["ntm_spd"][zt - 2])
                tb_ = b.op("vector", lambda e, slot=slot: e.tensor_tensor(
                    out=ntm[slot], in0=tf(3), in1=bb_bc[:], op=ALU.add))
                state["xnb_rd"][zt] = tb_
                b.wait("scalar", "VE", tb_)
                state["ntm_spd"][zt] = sp_nt.spill(zt, sc_ntm[zt], ntm[slot])
                b.wait("tensor", "VE", tb_)
                if zt >= 2:
                    b.wait("tensor", "VE", state["pe_free"][zt - 2])
                tok_tp = None
                for j in range(NC):
                    tok_tp = b.op("tensor", lambda e, j=j, slot=slot: e.transpose(
                        PE[slot][:, j * 128:(j + 1) * 128], ntm[slot][:, j * 128:(j + 1) * 128],
                        ident_b[:]))
                state["ntm_tpd"][zt] = tok_tp
                b.wait("vector", "TE", tok_tp)
                state["pe_free"][zt] = b.op("vector", lambda e, slot=slot, zt=zt: e.tensor_copy(
                    out=nfm_view[:, :, zt * 128:(zt + 1) * 128],
                    in_=PE[slot][:].rearrange("p (c z) -> p c z", c=NC)))
                wc.pump(2, allow_ffn=False)

        def queue_group(k):
            tiles = list(range(4 * k, 4 * k + 4))
            for zt in tiles:
                jobs.append(lambda zt=zt: j_sweepA(zt))
            jobs.append(lambda k=k: batched_chain("1", k))
            for zt in tiles:
                jobs.append(lambda zt=zt: j_sweepB(zt))
            jobs.append(lambda k=k: batched_chain("2", k))
            for zt in tiles:
                jobs.append(lambda zt=zt: j_sweepC(zt))
            jobs.append(lambda k=k: state["pe_group"].__setitem__(k, state["pe_free"][4 * k + 3]))

        def p1_pump(n):
            c = 0
            while jobs and c < n:
                jobs.pop(0)()
                c += 1

        return group, queue_group, p1_pump, state

    class WConv:

        def __init__(self):
            self.wf32 = [TB3[:, 14336 + i * 2048:14336 + (i + 1) * 2048].bitcast(F32)
                         for i in range(4)]
            self.ob2 = [TB3[:, 22528 + i * 1024:22528 + (i + 1) * 1024] for i in range(2)]
            self.fi32 = [TB3[:, 6144 + i * 2048:6144 + (i + 1) * 2048].bitcast(I32)
                         for i in range(2)]
            self.fbf = [TB3[:, 10240 + i * 1024:10240 + (i + 1) * 1024] for i in range(2)]
            self.ftr = [TB3[:, 12288 + i * 1024:12288 + (i + 1) * 1024] for i in range(2)]
            self.ld = Ring(b, "RWc", self.wf32)
            self.fld = Ring(b, "RFc", self.fi32)
            self.sp_w = SpillSlots(b, "SpWc", 2)
            self.sp_f = SpillSlots(b, "SpFc", 2)
            self.gates = {}
            self.last_sp = {}
            self.jobs = []
            self.aidx = 0
            self.fidx = 0
            self.wconv_done = {}
            self.fsp_done = {}
            self.ftr_done = {}
            self.pe_evac = {}
            self.pe_seed = None
            for p in ("h1", "h2", "fm"):
                for c in range(NC):
                    for q in range(4):
                        self.jobs.append((p + "_w1",
                                          aw[p + "_w1"][c * 128:(c + 1) * 128, q * 1024:(q + 1) * 1024],
                                          cw1[p][c][:, q * 1024:(q + 1) * 1024], False))
                for i in range(32):
                    self.jobs.append((p + "_w2", aw[p + "_w2"][i * 128:(i + 1) * 128, :],
                                      ("jsliced", cw2[p][:, i].transpose([1, 0, 2])), False))
            for t in range(NH):
                self.jobs.append(("ffn", w_gate[t * 128:(t + 1) * 128, :],
                                  cw_wgT[:, :, t * 128:(t + 1) * 128].transpose([1, 0, 2]), True))
            for t in range(NH):
                self.jobs.append(("ffn", w_up[t * 128:(t + 1) * 128, :],
                                  cw_wuT[:, :, t * 128:(t + 1) * 128].transpose([1, 0, 2]), True))
            for t in range(NC):
                for c3 in range(3):
                    self.jobs.append(("ffn", w_down[t * 128:(t + 1) * 128, c3 * 1024:(c3 + 1) * 1024],
                                      cw_wdT[c3 * 8:(c3 + 1) * 8, :, t * 128:(t + 1) * 128].transpose([1, 0, 2]), True))
            self.pos = 0

        def _attn_job(self, tag, srcap, dstap):
            idx = self.aidx
            self.aidx += 1
            buf, sn, tl = self.ld.load(idx, srcap, eng="gpsimd")
            slot = idx % 2
            ob = self.ob2[slot]
            buf1 = buf[:, 0:1024]
            b.wait("vector", sn, tl)
            if idx >= 2:
                b.wait("vector", *self.wconv_done[idx - 2])
            tv = b.op("vector", lambda e, buf1=buf1, ob=ob: e.tensor_copy(out=ob, in_=buf1))
            self.ld.set_free(idx, "VE", tv)
            b.wait("scalar", "VE", tv)
            if isinstance(dstap, tuple):
                sp = self.sp_w.spill(idx, dstap[1],
                                     ob.rearrange("p (j c) -> p j c", j=8))
            else:
                sp = self.sp_w.spill(idx, dstap, ob)
            self.wconv_done[idx] = sp
            self.last_sp.setdefault(tag, []).append(sp)

        def _ffn_job(self, tag, srcap, dstap):
            k = self.fidx
            self.fidx += 1
            buf, sn, tl = self.fld.load(k, srcap, eng="gpsimd")
            slot = k % 2
            b.wait("vector", sn, tl)
            if k >= 2:
                b.wait("vector", "TE", self.ftr_done[k - 2])
            tc = b.op("vector", lambda e, buf=buf, o=self.fbf[slot]: e.tensor_scalar(
                out=o, in0=buf, scalar1=1.0 / 127.0, scalar2=None, op0=ALU.mult))
            self.fld.set_free(k, "VE", tc)
            b.wait("tensor", "VE", tc)
            if k >= 2:
                b.wait("tensor", "VE", self.pe_evac[k - 2])
            elif self.pe_seed is not None:
                b.wait("tensor", "VE", self.pe_seed)
            tp = None
            for j in range(NC):
                tp = b.op("tensor", lambda e, j=j, slot=slot: e.transpose(
                    PE[slot][:, j * 128:(j + 1) * 128],
                    self.fbf[slot][:, j * 128:(j + 1) * 128], ident_b[:]))
            self.ftr_done[k] = tp
            b.wait("vector", "TE", tp)
            if k >= 2:
                b.wait("vector", *self.fsp_done[k - 2])
            tv = b.op("vector", lambda e, slot=slot: e.tensor_copy(
                out=self.ftr[slot].rearrange("p (j c) -> p j c", j=8),
                in_=PE[slot][:].rearrange("p (j c) -> p j c", j=8)))
            self.pe_evac[k] = tv
            b.wait("scalar", "VE", tv)
            sp = self.sp_f.spill(k, dstap, self.ftr[slot].rearrange("p (j c) -> p j c", j=8))
            self.fsp_done[k] = sp
            self.last_sp.setdefault("ffn", []).append(sp)

        def pump(self, n, allow_ffn=True):
            done = 0
            while done < n and self.pos < len(self.jobs):
                tag, srcap, dstap, is_ffn = self.jobs[self.pos]
                if is_ffn and not allow_ffn:
                    return done
                if is_ffn and self.pe_seed is None:
                    self.pe_seed = b.cur("VE")
                self.pos += 1
                if is_ffn:
                    self._ffn_job(tag, srcap, dstap)
                else:
                    self._attn_job(tag, srcap, dstap)
                done += 1
            return done

        def gate(self, tag):
            if tag not in self.gates:
                while self.pos < len(self.jobs) and any(
                        j[0] == tag for j in self.jobs[self.pos:]):
                    self.pump(1)
                for sp in self.last_sp[tag][-2:]:
                    b.wait("scalar", *sp)
                self.gates[tag] = b.op("scalar", lambda e: e.nop())
            return self.gates[tag]

    def mlp_builder(p, wc, epilogue, rhs_wait=None, pump=None):
        rhs_view = TB1[:, 0:NC * S].rearrange("p (c z) -> p c z", c=NC)
        w1s = [TB2[:, 16384 + i * 4096:16384 + (i + 1) * 4096].rearrange("p (c w) -> p c w", c=NC)
               for i in range(2)]
        w2s = [TB2[:, 24576 + i * 4096:24576 + (i + 1) * 4096].rearrange("p (i c) -> p i c", i=32)
               for i in range(2)]
        h_sb = TB2[:, 0:16384]
        r_w1 = Ring(b, "Rw1" + p, w1s)
        r_w2 = Ring(b, "Rw2" + p, w2s)
        g1 = wc.gate(p + "_w1")
        g2 = wc.gate(p + "_w2")
        b.wait("sync", "SE", g1)
        b.wait("sync", "SE", g2)
        PH = [PA[:, 0:512], PA[:, 512:1024], PB[:, 0:512], PB[:, 512:1024]]
        PO = [PC[:, 0:512], PC[:, 512:1024]]
        mst = {"ph_free": {}, "po_free": {}, "hseq": 0, "oseq": 0, "hsb_war": None}

        def slice_fn(zs):
            ph_free, po_free = mst["ph_free"], mst["po_free"]
            hseq, oseq = mst["hseq"], mst["oseq"]
            if rhs_wait is not None:
                rhs_wait(zs)
            last_evac = None
            for q in range(8):
                w1buf, sn1, t1 = r_w1.load(zs * 8 + q, cw1[p][:, :, q * 512:(q + 1) * 512].transpose([1, 0, 2]))
                b.wait("tensor", sn1, t1)
                for dl in range(4):
                    di = q * 4 + dl
                    rot = hseq % 4
                    hseq += 1
                    if rot in ph_free:
                        b.wait("tensor", "SE", ph_free[rot])
                    pt = None
                    for c in range(NC):
                        pt = b.op("tensor", lambda e, c=c, dl=dl, w1buf=w1buf, rot=rot, zs=zs: e.matmul(
                            PH[rot], w1buf[:, c, dl * 128:(dl + 1) * 128],
                            rhs_view[:, c, zs * 512:(zs + 1) * 512],
                            start=(c == 0), stop=(c == NC - 1)))
                    r_w1.set_free(zs * 8 + q, "TE", pt)
                    if pump is not None:
                        pump(1)
                    b.wait("scalar", "TE", pt)
                    if mst["hsb_war"] is not None:
                        b.wait("scalar", "TE", mst["hsb_war"])
                    last_evac = b.op("scalar", lambda e, rot=rot, di=di: e.activation(
                        out=h_sb[:, di * 512:(di + 1) * 512], in_=PH[rot], func=AF.Gelu,
                        bias=bias1[p][:, di:di + 1]))
                    ph_free[rot] = last_evac
            pb_last = None
            for j in range(NC):
                w2buf, sn2, t2 = r_w2.load(zs * 8 + j, cw2[p][j].transpose([1, 0, 2]))
                b.wait("tensor", sn2, t2)
                rot = oseq % 2
                oseq += 1
                if rot in po_free:
                    b.wait("tensor", "SE", po_free[rot])
                b.wait("tensor", "SE", last_evac)
                pt = None
                for i in range(32):
                    pt = b.op("tensor", lambda e, i=i, w2buf=w2buf, rot=rot: e.matmul(
                        PO[rot], w2buf[:, i, :], h_sb[:, i * 512:(i + 1) * 512],
                        start=(i == 0), stop=(i == 31)))
                r_w2.set_free(zs * 8 + j, "TE", pt)
                pb_last = pt
                if pump is not None:
                    pump(1)
                tok_ep = epilogue(j, zs, PO[rot], pt)
                po_free[rot] = tok_ep
            mst["hseq"], mst["oseq"], mst["hsb_war"] = hseq, oseq, pb_last

        return slice_fn, mst

    def emit_P23(wc, p1_group, p1_queue, p1_pump, p1_state):
        outbuf = [tf(4).bitcast(BF16)[:, 0:512], tf(4).bitcast(BF16)[:, 512:1024]]
        sp_o = SpillSlots(b, "SpW" + "x", 2)
        state = {"n": 0, "free": {}}

        def mk_epilogue(dst):
            def epilogue(j, zs, psum, pt):
                k = state["n"]
                state["n"] += 1
                if k == 0:
                    b.wait("scalar", "VE", b.cur("VE"))
                slot = k % 2
                b.wait("scalar", "TE", pt)
                if slot in state["free"]:
                    b.wait("scalar", *state["free"][slot])
                bb = bias2[cur_p[0]][:, j:j + 1]
                te = b.op("scalar", lambda e, psum=psum, slot=slot, bb=bb: e.activation(
                    out=outbuf[slot], in_=psum, func=AF.Identity, bias=bb))
                b.wait("scalar", "SE", te)
                sp = sp_o.spill(k, dst[j][:, zs * 512:(zs + 1) * 512], outbuf[slot])
                state["free"][slot] = sp
                return te
            return epilogue

        cur_p = ["h1"]

        def rhs_wait_h1(zs):
            b.wait("tensor", "VE", p1_state["pe_group"][zs])

        h1_slice, h1_mst = mlp_builder("h1", wc, mk_epilogue(sc_w1), rhs_wait_h1,
                                       pump=p1_pump)
        p1_group(0)
        p1_group(1)
        for k in range(S // 512):
            if k + 2 < S // 512:
                p1_queue(k + 2)
            h1_slice(k)
            p1_pump(10 ** 9)
        wc.pump(10 ** 9)
        cur_p[0] = "h2"
        h2_slice, h2_mst = mlp_builder("h2", wc, mk_epilogue(sc_w2))
        for k in range(S // 512):
            h2_slice(k)
        return h2_mst["hsb_war"]

    def emit_P4(p23_done):
        b.sem("PL4")
        b.wait("gpsimd", "TE", p23_done)
        for sn in ("SpWx0", "SpWx1", "SpNt0", "SpNt1", "SpXn0", "SpXn1", "SpXn2", "SpXn3"):
            b.wait("gpsimd", sn, b.cur(sn))
        w2v = TB1[:, 0:NC * S].rearrange("p (c z) -> p c z", c=NC)
        pl_toks = []
        for c in range(NC):
            pl_toks.append(b.dma("gpsimd", w2v[:, c, :], sc_w2[c], "PL4"))
        ntm_v = TB2[:, 0:NT * D].rearrange("p (t f) -> p t f", t=NT)
        for t in range(NT):
            pl_toks.append(b.dma("gpsimd", ntm_v[:, t, :], sc_ntm[t], "PL4"))
        pl_tok = pl_toks[-1]

        w1slab = [TB3[:, 0:2048].rearrange("p (c z) -> p c z", c=NC),
                  TB3[:, 2048:4096].rearrange("p (c z) -> p c z", c=NC)]
        w3sb = [TB3[:, 4096 + i * 256:4096 + (i + 1) * 256] for i in range(3)]
        xnr = [TB3[:, 4864 + i * 1024:4864 + (i + 1) * 1024] for i in range(2)]
        uring = [TB3[:, 6912 + i * 1024:6912 + (i + 1) * 1024] for i in range(8)]
        x1ring = [TB3[:, 15104 + i * 1024:15104 + (i + 1) * 1024] for i in range(8)]
        r_w1s = Ring(b, "Rw1s", [w1slab[0], w1slab[1]])
        r_xn = Ring(b, "Rxn4", xnr)
        sp_x1 = SpillSlots(b, "SpX1", 2)
        x1nfm_sp = {}

        PT = [PA, PB]
        PW3 = [PC[:, 0:512], PC[:, 512:1024]]
        w3_ev = {}
        w3_rd = {}
        pt_ev = {}
        pe_ev = {}
        u_rd = {}
        x1_tp = {}
        NB = 4
        b.wait("tensor", "VE", b.cur("VE"))
        b.wait("sync", "TE", p23_done)
        for sn in ("SpXn0", "SpXn1", "SpXn2", "SpXn3", "SpNt0", "SpNt1",
                   "SpWc0", "SpWc1", "SpFc0", "SpFc1", "SpWx0", "SpWx1"):
            b.wait("sync", sn, b.cur(sn))
        b.wait("sync", "VE", b.cur("VE"))
        b.wait("sync", "GE", b.cur("GE"))
        b.wait("vector", "SpWx0", b.cur("SpWx0"))
        b.wait("vector", "SpWx1", b.cur("SpWx1"))

        def uchain_job(base):
            b.wait("vector", "SE", b.cur("SE"))
            b.drain("vector")
            b.op("vector", lambda e: e.tensor_scalar(out=st4["mu"][:], in0=st4["su"][:],
                 scalar1=1.0 / D, scalar2=None, op0=ALU.mult))
            b.drain("vector")
            b.op("vector", lambda e: e.tensor_tensor(out=st4["vu"][:], in0=st4["mu"][:],
                 in1=st4["mu"][:], op=ALU.mult))
            b.drain("vector")
            tvu = b.op("vector", lambda e: e.scalar_tensor_tensor(out=st4["vu"][:],
                 in0=st4["qu"][:], scalar=1.0 / D, in1=st4["vu"][:],
                 op0=ALU.mult, op1=ALU.subtract))
            b.wait("scalar", "VE", tvu)
            tsd = b.op("scalar", lambda e: e.activation(out=st4["ru"][:], in_=st4["vu"][:],
                 func=AF.Sqrt, bias=c_eps[:]))
            b.wait("vector", "SE", tsd)
            b.op("vector", lambda e: e.reciprocal(out=st4["ru"][:], in_=st4["ru"][:]))
            b.drain("vector")
            tnb = b.op("vector", lambda e: e.scalar_tensor_tensor(out=st4["nbu"][:],
                 in0=st4["mu"][:], scalar=-1.0, in1=st4["ru"][:],
                 op0=ALU.mult, op1=ALU.mult))
            b.wait("scalar", "VE", tnb)

        def x1_job(base, i):
            g128 = base + i
            xnb_, snx, tlx = r_xn.load(g128, sc_xn[g128], eng="gpsimd")
            u_rd[i] = tno = b.op("scalar", lambda e, i=i: e.activation(
                out=tf(0), in_=uring[i], func=AF.Identity,
                bias=st4["nbu"][:, i:i + 1], scale=st4["ru"][:, i:i + 1]))
            b.wait("vector", "SE", tno)
            b.op("vector", lambda e: e.tensor_tensor(out=tf(1), in0=tf(0), in1=g_bc[:], op=ALU.mult))
            b.op("vector", lambda e: e.tensor_tensor(out=tf(1), in0=tf(1), in1=bb_bc[:], op=ALU.add))
            b.wait("vector", snx, tlx)
            if i in x1_tp:
                b.wait("vector", "TE", x1_tp[i])
            tx1 = b.op("vector", lambda e, xnb_=xnb_, i=i: e.scalar_tensor_tensor(
                out=x1ring[i], in0=tf(1), scalar=1.0, in1=xnb_,
                op0=ALU.mult, op1=ALU.add, accum_out=st4["sx"][:, i:i + 1]))
            r_xn.set_free(g128, "VE", tx1)
            b.wait("scalar", "VE", tx1)
            b.op("scalar", lambda e, i=i: e.activation(
                out=tf(4), in_=x1ring[i], func=AF.Square,
                accum_out=st4["qx"][:, i:i + 1]))

        def x1chain_job(base):
            b.wait("vector", "SE", b.cur("SE"))
            b.drain("vector")
            b.op("vector", lambda e: e.tensor_scalar(out=st4["mx"][:], in0=st4["sx"][:],
                 scalar1=1.0 / D, scalar2=None, op0=ALU.mult))
            b.drain("vector")
            b.op("vector", lambda e: e.tensor_tensor(out=st4["vx"][:], in0=st4["mx"][:],
                 in1=st4["mx"][:], op=ALU.mult))
            b.drain("vector")
            tvx = b.op("vector", lambda e: e.scalar_tensor_tensor(out=st4["vx"][:],
                 in0=st4["qx"][:], scalar=1.0 / D, in1=st4["vx"][:],
                 op0=ALU.mult, op1=ALU.subtract))
            b.wait("scalar", "VE", tvx)
            tsdx = b.op("scalar", lambda e: e.activation(out=st4["rx"][:], in_=st4["vx"][:],
                 func=AF.Sqrt, bias=c_eps[:]))
            b.wait("vector", "SE", tsdx)
            b.op("vector", lambda e: e.reciprocal(out=st4["rx"][:], in_=st4["rx"][:]))
            b.drain("vector")
            tnbx = b.op("vector", lambda e: e.scalar_tensor_tensor(out=st4["nbx"][:],
                 in0=st4["mx"][:], scalar=-1.0, in1=st4["rx"][:],
                 op0=ALU.mult, op1=ALU.mult))
            b.wait("scalar", "VE", tnbx)

        def x1n_job(base, i):
            g128 = base + i
            tno = b.op("scalar", lambda e, i=i: e.activation(
                out=tf(0), in_=x1ring[i], func=AF.Identity,
                bias=st4["nbx"][:, i:i + 1], scale=st4["rx"][:, i:i + 1]))
            b.wait("vector", "SE", tno)
            b.op("vector", lambda e: e.tensor_tensor(out=tf(1), in0=tf(0), in1=g_bc[:], op=ALU.mult))
            tx1n = b.op("vector", lambda e, i=i: e.tensor_tensor(
                out=x1ring[i], in0=tf(1), in1=bb_bc[:], op=ALU.add))
            b.wait("tensor", "VE", tx1n)
            pes = i % 2
            if pes in pe_ev:
                b.wait("tensor", "VE", pe_ev[pes])
            ttp = None
            for j in range(NC):
                ttp = b.op("tensor", lambda e, j=j, pes=pes, i=i: e.transpose(
                    PE[pes][:, j * 128:(j + 1) * 128],
                    x1ring[i][:, j * 128:(j + 1) * 128], ident_b[:]))
            x1_tp[i] = ttp
            b.wait("vector", "TE", ttp)
            pe_ev[pes] = b.op("vector", lambda e, pes=pes, i=i: e.tensor_copy(
                out=uring[i].rearrange("p (c z) -> p c z", c=NC),
                in_=PE[pes][:].rearrange("p (c z) -> p c z", c=NC)))
            b.wait("scalar", "VE", pe_ev[pes])
            x1nfm_sp[i] = sp_x1.spill(
                g128, sc_x1nfm[:, :, g128 * 128:(g128 + 1) * 128].transpose([1, 0, 2]),
                uring[i].rearrange("p (c z) -> p c z", c=NC))

        pending = []

        def pump_jobs(n):
            cnt = 0
            while pending and cnt < n:
                pending.pop(0)()
                cnt += 1

        w3seq = 0
        for zt4 in range(S // 256):
            slab, sns, tls = r_w1s.load(zt4, sc_w1[:, :, zt4 * 256:(zt4 + 1) * 256].transpose([1, 0, 2]))
            b.wait("tensor", sns, tls)
            b.wait("tensor", "PL4", pl_tok)
            nsb = 2 * zt4 + 2
            w3q = []

            def emit_w3(sb):
                rot = w3seq_l[0] % 2
                w3seq_l[0] += 1
                if rot in w3_ev:
                    b.wait("tensor", "VE", w3_ev[rot])
                tw = None
                for c in range(NC):
                    tw = b.op("tensor", lambda e, c=c, sb=sb, rot=rot, slab=slab: e.matmul(
                        PW3[rot][:, 0:256], w2v[:, c, sb * 128:(sb + 1) * 128],
                        slab[:, c, :], start=(c == 0), stop=(c == NC - 1)))
                b.wait("vector", "TE", tw)
                slot = w3seq_l[0] % 3
                if slot in w3_rd:
                    b.wait("vector", "TE", w3_rd[slot])
                if sb == 2 * zt4:
                    mask = maskA
                elif sb == 2 * zt4 + 1:
                    mask = maskB
                else:
                    mask = None
                if mask is not None:
                    w3_ev[rot] = b.op("vector", lambda e, rot=rot, slot=slot, mask=mask: e.tensor_tensor(
                        out=w3sb[slot], in0=PW3[rot][:, 0:256], in1=mask[:], op=ALU.mult))
                else:
                    w3_ev[rot] = b.op("vector", lambda e, rot=rot, slot=slot: e.tensor_copy(
                        out=w3sb[slot], in_=PW3[rot][:, 0:256]))
                return (sb, rot, slot, w3_ev[rot])

            def emit_t(sb, slot, evtok):
                b.wait("tensor", "VE", evtok)
                if sb == 0:
                    for zsub in range(2):
                        if zsub in pt_ev:
                            b.wait("tensor", "SE", pt_ev[zsub])
                tt = None
                stop_ = (sb == nsb - 1)
                for zsub in range(2):
                    for hh in range(2):
                        tt = b.op("tensor", lambda e, zsub=zsub, hh=hh, sb=sb, slot=slot, stop_=stop_: e.matmul(
                            PT[zsub][:, hh * 512:(hh + 1) * 512],
                            w3sb[slot][:, zsub * 128:(zsub + 1) * 128],
                            ntm_v[:, sb, hh * 512:(hh + 1) * 512],
                            start=(sb == 0), stop=stop_,
                            skip_group_check=True))
                w3_rd[slot] = tt
                return tt

            w3seq_l = [w3seq]
            from collections import deque
            pend = deque()
            pend.append(emit_w3(0))
            if nsb > 1:
                pend.append(emit_w3(1))
            for sb in range(2, nsb):
                nxt = emit_w3(sb)
                p0 = pend.popleft()
                tt = emit_t(p0[0], p0[2], p0[3])
                pump_jobs(2)
                pend.append(nxt)
            while pend:
                p0 = pend.popleft()
                tt = emit_t(p0[0], p0[2], p0[3])
                pump_jobs(1)
            w3seq = w3seq_l[0]
            r_w1s.set_free(zt4, "TE", tt)
            for zsub in range(2):
                i128 = 2 * zt4 + zsub
                col = i128 % 8
                b.wait("scalar", "TE", tt)
                if col in u_rd:
                    b.wait("scalar", "SE", u_rd[col])
                if col in x1nfm_sp:
                    b.wait("scalar", *x1nfm_sp[col])
                pt_ev[zsub] = b.op("scalar", lambda e, zsub=zsub, col=col: e.activation(
                    out=uring[col], in_=PT[zsub][:], func=AF.Gelu,
                    accum_out=st4["su"][:, col:col + 1]))
                b.op("scalar", lambda e, col=col: e.activation(
                    out=tf(4), in_=uring[col], func=AF.Square,
                    accum_out=st4["qu"][:, col:col + 1]))
            if zt4 % NB == NB - 1:
                base = (zt4 // NB) * 8
                assert not pending
                pending.append(lambda base=base: uchain_job(base))
                for i in range(8):
                    pending.append(lambda base=base, i=i: x1_job(base, i))
                pending.append(lambda base=base: x1chain_job(base))
                for i in range(8):
                    pending.append(lambda base=base, i=i: x1n_job(base, i))
        pump_jobs(10 ** 9)
        b.wait("scalar", *x1nfm_sp[6])
        b.wait("scalar", *x1nfm_sp[7])
        return b.op("scalar", lambda e: e.nop())

    def emit_P5(wc, p4_done):
        b.sem("PL5")
        b.wait("gpsimd", "SE", p4_done)
        b.wait("gpsimd", "TE", b.cur("TE"))
        for sn in ("SpX10", "SpX11"):
            b.wait("gpsimd", sn, b.cur(sn))
        rhs_view = TB1[:, 0:NC * S].rearrange("p (c z) -> p c z", c=NC)
        pl = None
        for c in range(NC):
            pl = b.dma("gpsimd", rhs_view[:, c, :], sc_x1nfm[c], "PL5")
        fmv = TB3[:, 0:4096].rearrange("p (j z) -> p j z", j=NC)
        r_x = Ring(b, "Rx5", [tf(2), tf(3)])
        r_xn5 = Ring(b, "Rxn5", [TB3[:, 4096 + i * 1024:4096 + (i + 1) * 1024] for i in range(2)])
        sp_at = SpillSlots(b, "SpAt", 2)
        b.wait("scalar", "VE", b.cur("VE"))
        b.wait("scalar", "TE", b.cur("TE"))
        for sn in ("SpX10", "SpX11", "SpWx0", "SpWx1"):
            b.wait("scalar", sn, b.cur(sn))
        for i in range(2):
            r_x.set_free(i, "SE", b.cur("SE"))
            r_xn5.set_free(i, "SE", b.cur("SE"))
        st5 = {"tp_last": None, "pe_ev": {}, "at_sp": {}, "ep_done": {}}
        b.wait("tensor", "PL5", pl)

        def ep5(j, zs, psum, pt):
            b.wait("scalar", "TE", pt)
            if st5["tp_last"] is not None:
                b.wait("scalar", "TE", st5["tp_last"])
            te = b.op("scalar", lambda e, psum=psum, j=j: e.activation(
                out=fmv[:, j, :], in_=psum, func=AF.Identity,
                bias=bias2["fm"][:, j:j + 1]))
            st5["ep_done"][j] = te
            if j == NC - 1:
                for zloc in range(4):
                    g128 = zs * 4 + zloc
                    xb, snx, tlx = r_x.load(g128, x_in[g128 * 128:(g128 + 1) * 128, :], eng="sync")
                    xnb_, snn, tln = r_xn5.load(g128, sc_xn[g128], eng="gpsimd")
                    pes = zloc % 2
                    b.wait("tensor", "SE", te)
                    if pes in st5["pe_ev"]:
                        b.wait("tensor", "VE", st5["pe_ev"][pes])
                    ttp = None
                    for jj in range(NC):
                        ttp = b.op("tensor", lambda e, jj=jj, zloc=zloc, pes=pes: e.transpose(
                            PE[pes][:, jj * 128:(jj + 1) * 128],
                            fmv[:, jj, zloc * 128:(zloc + 1) * 128], ident_b[:]))
                    st5["tp_last"] = ttp
                    outsl = [tf(0), tf(1)][g128 % 2]
                    b.wait("vector", "TE", ttp)
                    b.wait("vector", snn, tln)
                    b.wait("vector", snx, tlx)
                    if g128 >= 2:
                        b.wait("vector", *st5["at_sp"][g128 - 2])
                    tv1 = b.op("vector", lambda e, pes=pes, xnb_=xnb_: e.scalar_tensor_tensor(
                        out=tf(4), in0=PE[pes][:], scalar=1.0, in1=xnb_,
                        op0=ALU.mult, op1=ALU.add))
                    st5["pe_ev"][pes] = tv1
                    r_xn5.set_free(g128, "VE", tv1)
                    tv2 = b.op("vector", lambda e, xb=xb, outsl=outsl: e.tensor_tensor(
                        out=outsl, in0=tf(4), in1=xb, op=ALU.add))
                    r_x.set_free(g128, "VE", tv2)
                    b.wait("scalar", "VE", tv2)
                    st5["at_sp"][g128] = sp_at.spill(g128, sc_attns[g128], outsl)
            return te

        fm_slice, fm_mst = mlp_builder("fm", wc, ep5)
        for k in range(S // 512):
            fm_slice(k)
        b.wait("scalar", *st5["at_sp"][S // 128 - 1])
        b.wait("scalar", *st5["at_sp"][S // 128 - 2])
        return b.op("scalar", lambda e: e.nop())

    def emit_P6(wc, p5_done):
        ZS6 = 256
        b.sem("PL6")
        b.wait("gpsimd", "SE", wc.gate("ffn"))
        for sn in ("SpX10", "SpX11"):
            b.wait("gpsimd", sn, b.cur(sn))
        b.wait("gpsimd", "TE", b.cur("TE"))
        b.wait("gpsimd", "VE", b.cur("VE"))
        wgv = TB3[:, 0:NC * H].rearrange("p (c h) -> p c h", c=NC)
        pl6 = []
        for c in range(NC):
            pl6.append(b.dma("gpsimd", wgv[:, c, :], cw_wgT[c], "PL6"))
        b.wait("gpsimd", "SE", p5_done)
        b.wait("gpsimd", "TE", b.cur("TE"))
        wuv = TB2[:, 0:NC * H].rearrange("p (c h) -> p c h", c=NC)
        for c in range(NC):
            pl6.append(b.dma("gpsimd", wuv[:, c, :], cw_wuT[c], "PL6"))
        wdv = TB1[:, 0:NH * D].rearrange("p (i d) -> p i d", i=NH)
        for i in range(NH):
            pl6.append(b.dma("gpsimd", wdv[:, i, :], cw_wdT[i], "PL6"))
        pl_tok = pl6[-1]
        h_sb = TB2[:, 24576:24576 + NH * ZS6]
        sg = [TB2[:, 30720:30720 + ZS6], TB2[:, 30720 + ZS6:30720 + 2 * ZS6]]
        xnfm_sl = TB1[:, 24576:24576 + NC * ZS6].rearrange("p (c z) -> p c z", c=NC)
        r_xn6 = Ring(b, "Rxn6", [TB1[:, 26624 + i * 1024:26624 + (i + 1) * 1024] for i in range(2)])
        r_at = Ring(b, "Rat6", [tf(2), tf(3)])
        for i in range(2):
            r_xn6.set_free(i, "TE", b.cur("TE"))
            r_at.set_free(i, "VE", b.cur("VE"))
        sp_y = SpillSlots(b, "SpY", 2)
        b.wait("tensor", "PL6", pl_tok)
        b.wait("tensor", "VE", b.cur("VE"))
        pe_ev = {}
        sg_rd = {}
        hsb_rd = None
        pg_ev = {}
        pu_ev = {}
        pc_ev = None
        y_sp = {}
        gseq = 0
        prev_xnfm_rd = None
        for zs in range(S // ZS6):
            tpl = None
            for tz in range(ZS6 // 128):
                g32 = zs * (ZS6 // 128) + tz
                xnb_, snn, tln = r_xn6.load(g32, sc_xn[g32], eng="gpsimd")
                if zs > 0 and tz == 0:
                    b.wait("vector", "TE", prev_xnfm_rd)
                pes = gseq % 2
                gseq += 1
                b.wait("tensor", snn, tln)
                if pes in pe_ev:
                    b.wait("tensor", "VE", pe_ev[pes])
                for j in range(NC):
                    tpl = b.op("tensor", lambda e, j=j, pes=pes, xnb_=xnb_: e.transpose(
                        PE[pes][:, j * 128:(j + 1) * 128],
                        xnb_[:, j * 128:(j + 1) * 128], ident_b[:]))
                r_xn6.set_free(g32, "TE", tpl)
                b.wait("vector", "TE", tpl)
                pe_ev[pes] = b.op("vector", lambda e, pes=pes, tz=tz: e.tensor_copy(
                    out=xnfm_sl[:, :, tz * 128:(tz + 1) * 128],
                    in_=PE[pes][:, 0:NC * 128].rearrange("p (c z) -> p c z", c=NC)))
            xnfm_ready = pe_ev[(gseq - 1) % 2]
            b.wait("tensor", "VE", xnfm_ready)
            xnfm_rd = None
            for i in range(NH):
                rotg = i % 2
                if rotg in pg_ev:
                    b.wait("tensor", "SE", pg_ev[rotg])
                tg = None
                for c in range(NC):
                    tg = b.op("tensor", lambda e, c=c, i=i, rotg=rotg: e.matmul(
                        PA[:, rotg * 512:rotg * 512 + ZS6], wgv[:, c, i * 128:(i + 1) * 128],
                        xnfm_sl[:, c, :], start=(c == 0), stop=(c == NC - 1)))
                b.wait("scalar", "TE", tg)
                if rotg in sg_rd:
                    b.wait("scalar", "VE", sg_rd[rotg])
                tsg = b.op("scalar", lambda e, rotg=rotg: e.activation(
                    out=sg[rotg], in_=PA[:, rotg * 512:rotg * 512 + ZS6], func=AF.Silu))
                pg_ev[rotg] = tsg
                if rotg in pu_ev:
                    b.wait("tensor", "VE", pu_ev[rotg])
                tu = None
                for c in range(NC):
                    tu = b.op("tensor", lambda e, c=c, i=i, rotg=rotg: e.matmul(
                        PB[:, rotg * 512:rotg * 512 + ZS6], wuv[:, c, i * 128:(i + 1) * 128],
                        xnfm_sl[:, c, :], start=(c == 0), stop=(c == NC - 1)))
                xnfm_rd = tu
                b.wait("vector", "TE", tu)
                b.wait("vector", "SE", tsg)
                if hsb_rd is not None:
                    b.wait("vector", "TE", hsb_rd)
                tm = b.op("vector", lambda e, rotg=rotg, i=i: e.tensor_tensor(
                    out=h_sb[:, i * ZS6:(i + 1) * ZS6],
                    in0=PB[:, rotg * 512:rotg * 512 + ZS6], in1=sg[rotg], op=ALU.mult))
                pu_ev[rotg] = tm
                sg_rd[rotg] = tm
                h_last = tm
            td = None
            for zloc in range(ZS6 // 128):
                g128 = zs * (ZS6 // 128) + zloc
                if pc_ev is not None:
                    b.wait("tensor", "VE", pc_ev)
                b.wait("tensor", "VE", h_last)
                for i in range(NH):
                    for hh in range(2):
                        td = b.op("tensor", lambda e, i=i, hh=hh, zloc=zloc: e.matmul(
                            PC[:, hh * 512:(hh + 1) * 512],
                            h_sb[:, i * ZS6 + zloc * 128:i * ZS6 + (zloc + 1) * 128],
                            wdv[:, i, hh * 512:(hh + 1) * 512],
                            start=(i == 0), stop=(i == NH - 1),
                            skip_group_check=True))
                atb, sna, tla = r_at.load(g128, sc_attns[g128], eng="sync")
                outsl = [tf(0), tf(1)][g128 % 2]
                b.wait("vector", "TE", td)
                b.wait("vector", sna, tla)
                if g128 >= 2:
                    b.wait("vector", *y_sp[g128 - 2])
                tvy = b.op("vector", lambda e, atb=atb, outsl=outsl: e.tensor_tensor(
                    out=outsl, in0=PC[:], in1=atb, op=ALU.add))
                r_at.set_free(g128, "VE", tvy)
                pc_ev = tvy
                b.wait("scalar", "VE", tvy)
                y_sp[g128] = sp_y.spill(g128, y_out[g128 * 128:(g128 + 1) * 128, :], outsl)
            hsb_rd = td
            prev_xnfm_rd = xnfm_rd

    t_gb = emit_P0() if "P0" in phases else None
    wc = WConv()
    p23_done = None
    if "P23" in phases:
        p1_group, p1_queue, p1_pump, p1_state = p1_builder(t_gb, wc)
        p23_done = emit_P23(wc, p1_group, p1_queue, p1_pump, p1_state)
    if "P4" in phases:
        p4_done = emit_P4(p23_done)
    if "P5" in phases:
        p5_done = emit_P5(wc, p4_done)
    if "P6" in phases:
        emit_P6(wc, p5_done)

    for sname in list(b.sems):
        if sname in ("TE", "VE", "SE", "GE"):
            continue
        b.wait("gpsimd", sname, b.cur(sname))

    with stack, nc.Block() as block:
        @block.sync
        def _(e):
            for f in b.prog["sync"]:
                f(e)

        @block.gpsimd
        def _(e):
            for f in b.prog["gpsimd"]:
                f(e)

        @block.vector
        def _(e):
            for f in b.prog["vector"]:
                f(e)

        @block.scalar
        def _(e):
            for f in b.prog["scalar"]:
                f(e)

        @block.tensor
        def _(e):
            for f in b.prog["tensor"]:
                f(e)

    return nc


_NC_CACHE = {}


def _get_program():
    if "nc" not in _NC_CACHE:
        _NC_CACHE["nc"] = build_program(debug_taps=(), phases=("P0", "P23", "P4", "P5", "P6"))
    return _NC_CACHE["nc"]


def kernel(L0_attn_x, L0_attn_ln_g, L0_attn_ln_b,
           L0_attn_h1_w1, L0_attn_h1_b1, L0_attn_h1_w2, L0_attn_h1_b2,
           L0_attn_h2_w1, L0_attn_h2_b1, L0_attn_h2_w2, L0_attn_h2_b2,
           L0_attn_fm_w1, L0_attn_fm_b1, L0_attn_fm_w2, L0_attn_fm_b2,
           L0_ffn_w_gate, L0_ffn_w_up, L0_ffn_w_down):
    from concourse.bass_utils import run_bass_kernel_spmd
    x = np.ascontiguousarray(np.asarray(L0_attn_x, dtype=np.float32))
    common = {
        "ln_g": np.ascontiguousarray(np.asarray(L0_attn_ln_g, dtype=np.float32)),
        "ln_b": np.ascontiguousarray(np.asarray(L0_attn_ln_b, dtype=np.float32)),
        "h1_w1": np.ascontiguousarray(np.asarray(L0_attn_h1_w1, dtype=np.float32)),
        "h1_b1": np.ascontiguousarray(np.asarray(L0_attn_h1_b1, dtype=np.float32)),
        "h1_w2": np.ascontiguousarray(np.asarray(L0_attn_h1_w2, dtype=np.float32)),
        "h1_b2": np.ascontiguousarray(np.asarray(L0_attn_h1_b2, dtype=np.float32)),
        "h2_w1": np.ascontiguousarray(np.asarray(L0_attn_h2_w1, dtype=np.float32)),
        "h2_b1": np.ascontiguousarray(np.asarray(L0_attn_h2_b1, dtype=np.float32)),
        "h2_w2": np.ascontiguousarray(np.asarray(L0_attn_h2_w2, dtype=np.float32)),
        "h2_b2": np.ascontiguousarray(np.asarray(L0_attn_h2_b2, dtype=np.float32)),
        "fm_w1": np.ascontiguousarray(np.asarray(L0_attn_fm_w1, dtype=np.float32)),
        "fm_b1": np.ascontiguousarray(np.asarray(L0_attn_fm_b1, dtype=np.float32)),
        "fm_w2": np.ascontiguousarray(np.asarray(L0_attn_fm_w2, dtype=np.float32)),
        "fm_b2": np.ascontiguousarray(np.asarray(L0_attn_fm_b2, dtype=np.float32)),
        "w_gate": np.ascontiguousarray(np.asarray(L0_ffn_w_gate, dtype=np.int32)),
        "w_up": np.ascontiguousarray(np.asarray(L0_ffn_w_up, dtype=np.int32)),
        "w_down": np.ascontiguousarray(np.asarray(L0_ffn_w_down, dtype=np.int32)),
    }
    in_maps = [dict(common, x=np.ascontiguousarray(x[bidx])) for bidx in range(8)]
    nc = _get_program()
    res = run_bass_kernel_spmd(nc, in_maps, core_ids=list(range(8)))
    return np.stack([r["y"] for r in res.results], axis=0)
